# version 10
# baseline (speedup 1.0000x reference)
import numpy as np
import ml_dtypes
import concourse.bass as bass
import concourse.mybir as mybir

F32 = mybir.dt.float32
BF16 = mybir.dt.bfloat16
ALU = mybir.AluOpType
AF = mybir.ActivationFunctionType
bf16 = ml_dtypes.bfloat16

D = 4096
DC = D // 128
PROJ = 18560
POC = PROJ // 128
GATE = 8192
GC = GATE // 128
CONV = 10240
CC = CONV // 128
INTER = 8192
IC = INTER // 128
NH, HD, SS, NG, K = 128, 64, 128, 8, 4
L = 128
EPS = 1e-5
DT_MIN, DT_MAX = 1e-3, 100.0
FFN1 = 4352
FOC = FFN1 // 128
ENGINES = ("sync", "vector", "scalar", "tensor", "gpsimd")


class Sem:
    __slots__ = ("h", "n", "name")

    def __init__(self, handle, name):
        self.h = handle
        self.n = 0
        self.name = name

    def inc(self, d):
        self.n += d
        return self.n


class R:
    __slots__ = ("wmarks", "rmarks", "name")

    def __init__(self, name):
        self.name = name
        self.wmarks = []
        self.rmarks = []


class Kb:
    def __init__(self, nc, stack):
        self.nc = nc
        self.st = stack
        self.q = {k: [] for k in ENGINES}
        self.waited = {k: {} for k in ENGINES}
        self._sems = {}
        self._res = {}
        self.pending = {k: [] for k in ENGINES}
        self.ms = {e: self.sem("ms_" + e) for e in ("vector", "scalar", "tensor", "gpsimd")}

    def sem(self, name):
        if name not in self._sems:
            h = self.st.enter_context(self.nc.semaphore(name))
            self._sems[name] = Sem(h, name)
        return self._sems[name]

    def res(self, name):
        if name not in self._res:
            self._res[name] = R(name)
        return self._res[name]

    def wait(self, eng, sem, val):
        if val <= 0:
            return
        if self.waited[eng].get(sem.name, -1) >= val:
            return
        self.waited[eng][sem.name] = val
        self.q[eng].append(lambda e, s=sem.h, v=val: e.wait_ge(s, v))

    def dep(self, eng, marks):
        best = {}
        for (sem, val) in marks:
            if best.get(sem.name, (None, -1))[1] < val:
                best[sem.name] = (sem, val)
        for sem, val in best.values():
            self.wait(eng, sem, val)

    def do(self, eng, fn, reads=(), writes=(), sem=None, d=1):
        for r in reads:
            self.dep(eng, r.wmarks)
        for w in writes:
            self.dep(eng, w.wmarks)
            self.dep(eng, w.rmarks)
        if sem is None:
            self.q[eng].append(fn)
            for r in reads:
                self.pending[eng].append(("r", r))
            for w in writes:
                self.pending[eng].append(("w", w))
            return None
        v = sem.inc(d)
        self.q[eng].append(lambda e, f=fn, s=sem.h, dd=d: f(e).then_inc(s, dd))
        m = (sem, v)
        for kind, r in self.pending[eng]:
            if kind == "r":
                r.rmarks.append(m)
            else:
                r.wmarks.append(m)
        self.pending[eng] = []
        for r in reads:
            r.rmarks.append(m)
        for w in writes:
            w.wmarks = [m]
            w.rmarks = []
        return m

    def cop(self, eng, fn, reads=(), writes=(), d=1):
        return self.do(eng, fn, reads, writes, sem=self.ms[eng], d=d)

    def barrier(self):
        for e in ("vector", "scalar", "tensor", "gpsimd"):
            if self.pending[e]:
                self.cop(e, lambda en: en.nop())
        for e in ENGINES:
            for s in self._sems.values():
                if s.n > 0:
                    self.wait(e, s, s.n)

    def playback(self, block):
        @block.sync
        def _(e):
            for fn in self.q["sync"]:
                fn(e)

        @block.vector
        def _(e):
            for fn in self.q["vector"]:
                fn(e)

        @block.scalar
        def _(e):
            for fn in self.q["scalar"]:
                fn(e)

        @block.tensor
        def _(e):
            for fn in self.q["tensor"]:
                fn(e)

        @block.gpsimd
        def _(e):
            for fn in self.q["gpsimd"]:
                fn(e)


def _ap(t, offset, dims):
    return bass.AP(t, offset, [list(d) for d in dims])


def build_program(T=2048, n_pairs=4, upto="all"):
    nc = bass.Bass("TRN2", target_bir_lowering=False)
    W = min(512, T)
    NT = T // L
    NTILE = T // W
    TH = T + 3

    ext = {}
    def ein(name, shape, dt=F32):
        ext[name] = nc.dram_tensor(name, shape, dt, kind="ExternalInput").ap()
        return ext[name]

    xin = ein("xin", [D, T])
    xhalo = ein("xhalo", [D, 3])
    win_t = ein("win_t", [D, PROJ])
    wout_t = ein("wout_t", [INTER, D])
    wffn1_t = ein("wffn1_t", [D, FFN1])
    bstack = ein("bstack", [128, D])
    convw = ein("convw", [128, CC * K])
    convb = ein("convb", [128, CC])
    dtbias = ein("dtbias", [NH, 1])
    acol = ein("acol", [NH, 1])
    dexp = ein("dexp", [128, IC])
    nwexp = ein("nwexp", [128, IC])
    bbase = ein("bbase", [128, DC])
    bthr = ein("bthr", [128, 1])
    tri = ein("tri", [L, L], BF16)
    mmat = ein("mmat", [8, 128], BF16)
    identf = ein("identf", [128, 128], F32)
    identb = ein("identb", [128, 128], BF16)
    isodd = ein("isodd", [128, 1])
    dtm = ein("dtm", [128, 128])
    onesb = ein("onesb", [128, 1], BF16)
    onesrow = ein("onesrow", [1, 128], BF16)

    out_t = nc.dram_tensor("out_t", [D, T], F32, kind="ExternalOutput").ap()

    scr = {}
    def sdram(name, shape, dt=F32):
        scr[name] = nc.dram_tensor(name, shape, dt).ap()
        return scr[name]

    s_gate = sdram("s_gate", [GATE, T], BF16)
    s_xbc = sdram("s_xbc", [CONV, TH], BF16)
    s_xfm = sdram("s_xfm", [INTER, T], BF16)
    s_bfm = sdram("s_bfm", [NG * SS, T], BF16)
    s_cfm = sdram("s_cfm", [NG * SS, T], BF16)
    s_cs = sdram("s_cs", [NT, NH * L], F32)
    s_gexp = sdram("s_gexp", [NT, NH * L], BF16)
    s_gexp2 = sdram("s_gexp2", [NT, NH * L], BF16)
    s_gexpt = sdram("s_gexpt", [T, NH], BF16)
    s_ypart = sdram("s_ypart", [NT, 128 * IC * L], F32)
    s_bcf = sdram("s_bcf", [NT, 128 * 16 * L], BF16)
    s_normed = sdram("s_normed", [INTER, T], BF16)
    s_att = sdram("s_att", [D, T], F32)
    s_x2 = sdram("s_x2", [D, T], F32)
    s_ffn = sdram("s_ffn", [D, T], F32)
    s_carry = sdram("s_carry", [128, INTER], F32)
    s_carryag = sdram("s_carryag", [256, INTER], F32)

    from contextlib import ExitStack
    with ExitStack() as st:
        kb = Kb(nc, st)
        rD = {name: kb.res("d_" + name) for name in scr}
        rD["xin"] = kb.res("d_xin")
        rD["xhalo"] = kb.res("d_xhalo")
        rD["out_t"] = kb.res("d_out_t")

        def sb(stk, name, shape, dt=F32):
            return stk.enter_context(nc.sbuf_tensor(name, shape, dt))

        pA = [st.enter_context(nc.psum_tensor(f"pA{i}", [128, 512], F32)) for i in range(4)]
        rpA = [kb.res(f"pA{i}") for i in range(4)]
        pB = [st.enter_context(nc.psum_tensor(f"pB{i}", [128, 1024], F32)) for i in range(2)]
        rpB = [kb.res(f"pB{i}") for i in range(2)]

        k_dtbias = sb(st, "k_dtbias", [128, 1]); k_acol = sb(st, "k_acol", [128, 1])
        k_dexp = sb(st, "k_dexp", [128, IC]); k_nwexp = sb(st, "k_nwexp", [128, IC])
        k_bbase = sb(st, "k_bbase", [128, DC]); k_bthr = sb(st, "k_bthr", [128, 1])
        k_convb = sb(st, "k_convb", [128, CC]); k_convw = sb(st, "k_convw", [128, CC * K])
        k_tri = sb(st, "k_tri", [L, L], BF16); k_mmat = sb(st, "k_mmat", [8, 128], BF16)
        k_identf = sb(st, "k_identf", [128, 128]); k_identb = sb(st, "k_identb", [128, 128], BF16)
        k_isodd = sb(st, "k_isodd", [128, 1]); k_onesb = sb(st, "k_onesb", [128, 1], BF16)
        k_dtm = sb(st, "k_dtm", [128, 128])
        k_eps = sb(st, "k_eps", [128, 1]); k_one = sb(st, "k_one", [128, 1])
        k_onesrow = sb(st, "k_onesrow", [1, 128], BF16)
        rK = kb.res("konst")

        trow = sb(st, "trow", [1, W]); rtrow = kb.res("trow")
        trow2 = sb(st, "trow2", [1, W]); rtrow2 = kb.res("trow2")
        trowb = sb(st, "trowb", [1, W], BF16); rtrowb = kb.res("trowb")
        trowb2 = sb(st, "trowb2", [1, W], BF16); rtrowb2 = kb.res("trowb2")
        tchunk = sb(st, "tchunk", [128, W]); rtchunk = kb.res("tchunk")
        HC = 16

        def alloc_ln_bufs(sk, tag, with_out=False):
            b = {}
            b["xtile"] = [sb(sk, f"xtile{i}" + tag, [128, HC * W]) for i in range(2)]
            b["rxtile"] = [kb.res(f"xtile{i}" + tag) for i in range(2)]
            b["xsq"] = sb(sk, "xsq" + tag, [128, HC * W], BF16); b["rxsq"] = kb.res("xsq" + tag)
            if with_out:
                b["xbfb"] = sb(sk, "xbfb" + tag, [128, HC * W], BF16); b["rxbfb"] = kb.res("xbfb" + tag)
            b["rsb"] = sb(sk, "rsb" + tag, [128, W]); b["rrsb"] = kb.res("rsb" + tag)
            b["msb"] = sb(sk, "msb" + tag, [128, W]); b["rmsb"] = kb.res("msb" + tag)
            b["cnt"] = [0]
            if with_out:
                b["xout"] = sb(sk, "xout" + tag, [128, HC * W]); b["rxout"] = kb.res("xout" + tag)
            return b

        def emit_ln(bufs, src_dram, rsrc, srcT, t0, w, nch, mode, out_sb=None, rout=None,
                    resid_dram=None, dst_dram=None, rdst=None, chunk_off=0):
            xbfb, rxbfb = bufs.get("xbfb"), bufs.get("rxbfb")
            xsq, rxsq = bufs["xsq"], bufs["rxsq"]
            rsbuf, rrsb = bufs["rsb"], bufs["rrsb"]
            msbuf, rmsb = bufs["msb"], bufs["rmsb"]
            xout, rxout = bufs.get("xout"), bufs.get("rxout")
            nhalf = (nch + HC - 1) // HC
            def half_load(hi):
                c0 = hi * HC
                ncc = min(HC, nch - c0)
                i = bufs["cnt"][0] % 2
                bufs["cnt"][0] += 1
                xt, rxt = bufs["xtile"][i], bufs["rxtile"][i]
                ap = _ap(src_dram.tensor, (chunk_off + c0) * 128 * srcT + t0,
                         [[srcT, 128], [128 * srcT, ncc], [1, w]])
                kb.do("sync", lambda e, s=ap, n=ncc, x=xt: e.dma_start(
                    out=x[:, :n * w].rearrange("p (c t) -> p c t", c=n), in_=s),
                    reads=(rsrc,), writes=(rxt,), sem=ld(xt.name), d=16)
                return ncc, xt, rxt
            for hi in range(nhalf):
                ncc, xt, rxt = half_load(hi)
                if mode == "bf16":
                    xb_t, rxb_t, xoff = out_sb, rout, hi * HC
                else:
                    xb_t, rxb_t, xoff = xbfb, rxbfb, 0
                kb.cop("scalar", lambda e, n=ncc, x=xt: e.activation(
                    xsq[:, :n * w], x[:, :n * w], AF.Square),
                    reads=(rxt,), writes=(rxsq,))
                kb.cop("vector", lambda e, n=ncc, x=xt, xb=xb_t, o=xoff: e.tensor_copy(
                    xb[:, o * w:(o + n) * w], x[:, :n * w]),
                    reads=(rxt,), writes=(rxb_t,))
                for kc in range(ncc):
                    kb.do("tensor", lambda e, k=kc, h0=hi, n=ncc, nh=nhalf, xb=xb_t, o=xoff: e.matmul(
                        pB[0][0:1, :w], k_onesb[:, 0:1], xb[:, (o + k) * w:(o + k + 1) * w],
                        start=(h0 == 0 and k == 0), stop=(h0 == nh - 1 and k == n - 1)),
                        reads=(rxb_t, rK), writes=(rpB[0],),
                        sem=(kb.ms["tensor"] if kc == ncc - 1 else None))
                for kc in range(ncc):
                    kb.do("tensor", lambda e, k=kc, h0=hi, n=ncc, nh=nhalf: e.matmul(
                        pB[1][0:1, :w], k_onesb[:, 0:1], xsq[:, k * w:(k + 1) * w],
                        start=(h0 == 0 and k == 0), stop=(h0 == nh - 1 and k == n - 1)),
                        reads=(rxsq, rK), writes=(rpB[1],),
                        sem=(kb.ms["tensor"] if kc == ncc - 1 else None))
            nd = float(nch * 128)
            kb.cop("vector", lambda e: e.tensor_scalar(
                trow[:, :w], pB[0][0:1, :w], 1.0 / nd, None, op0=ALU.mult),
                reads=(rpB[0],), writes=(rtrow,))
            kb.cop("vector", lambda e: e.tensor_tensor(
                trow2[:, :w], trow[:, :w], trow[:, :w], op=ALU.mult),
                reads=(rtrow,), writes=(rtrow2,))
            kb.cop("vector", lambda e: e.scalar_tensor_tensor(
                trow2[:, :w], pB[1][0:1, :w], 1.0 / nd, trow2[:, :w],
                op0=ALU.mult, op1=ALU.subtract),
                reads=(rpB[1], rtrow2), writes=(rtrow2,))
            kb.cop("scalar", lambda e: e.activation(
                trow2[:, :w], trow2[:, :w], AF.Sqrt, bias=k_eps[0:1, 0:1]),
                reads=(rtrow2, rK), writes=(rtrow2,))
            kb.cop("vector", lambda e: e.reciprocal(trow2[:, :w], trow2[:, :w]),
                reads=(rtrow2,), writes=(rtrow2,))
            kb.cop("vector", lambda e: e.tensor_copy(trowb[:, :w], trow2[:, :w]),
                reads=(rtrow2,), writes=(rtrowb,))
            kb.cop("vector", lambda e: e.tensor_tensor(
                trow[:, :w], trow[:, :w], trow2[:, :w], op=ALU.mult),
                reads=(rtrow, rtrow2), writes=(rtrow,))
            kb.cop("vector", lambda e: e.tensor_scalar(
                trowb2[:, :w], trow[:, :w], -1.0, None, op0=ALU.mult),
                reads=(rtrow,), writes=(rtrowb2,))
            kb.cop("tensor", lambda e: e.matmul(
                pA[2][:, :w], k_onesrow[:, :], trowb[:, :w], start=True, stop=True),
                reads=(rtrowb, rK), writes=(rpA[2],))
            kb.cop("tensor", lambda e: e.matmul(
                pA[3][:, :w], k_onesrow[:, :], trowb2[:, :w], start=True, stop=True),
                reads=(rtrowb2, rK), writes=(rpA[3],))
            kb.cop("vector", lambda e: e.tensor_copy(rsbuf[:, :w], pA[2][:, :w]),
                reads=(rpA[2],), writes=(rrsb,))
            kb.cop("vector", lambda e: e.tensor_copy(msbuf[:, :w], pA[3][:, :w]),
                reads=(rpA[3],), writes=(rmsb,))
            for hi in range(nhalf):
                ncc, xt, rxt = half_load(hi)
                for kc in range(ncc):
                    eng = "vector" if (kc % 3) else "gpsimd"
                    if mode == "bf16":
                        dst_sb, rdst_sb, off = out_sb, rout, (hi * HC + kc)
                    else:
                        dst_sb, rdst_sb, off = xout, rxout, kc
                    kb.cop(eng, lambda e, k=kc, x=xt, ds=dst_sb, o=off: e.tensor_tensor(
                        ds[:, o * w:(o + 1) * w],
                        x[:, k * w:(k + 1) * w], rsbuf[:, :w], op=ALU.mult),
                        reads=(rxt, rrsb), writes=(rdst_sb,))
                    kb.cop(eng, lambda e, ds=dst_sb, o=off: e.tensor_tensor(
                        ds[:, o * w:(o + 1) * w], ds[:, o * w:(o + 1) * w], msbuf[:, :w], op=ALU.add),
                        reads=(rdst_sb, rmsb), writes=(rdst_sb,))
                if mode == "resid":
                    c0 = hi * HC
                    rap = _ap(resid_dram[0].tensor, c0 * 128 * srcT + t0,
                              [[srcT, 128], [128 * srcT, ncc], [1, w]])
                    kb.do("gpsimd", lambda e, s=rap, n=ncc: e.dma_start(
                        out=xout[:, :n * w].rearrange("p (c t) -> p c t", c=n), in_=s,
                        accum_op=ALU.add),
                        reads=(resid_dram[1], rxout), writes=(rxout,), sem=ld("xout"), d=16)
                    dap = _ap(dst_dram.tensor, c0 * 128 * srcT + t0,
                              [[srcT, 128], [128 * srcT, ncc], [1, w]])
                    kb.do("sync", lambda e, dd=dap, n=ncc: e.dma_start(
                        out=dd, in_=xout[:, :n * w].rearrange("p (c t) -> p c t", c=n)),
                        reads=(rxout,), writes=(rdst,), sem=sst("xout"), d=16)

        sload = {}
        def ld(name):
            if name not in sload:
                sload[name] = kb.sem("ld_" + name)
            return sload[name]
        sstore = {}
        def sst(name):
            if name not in sstore:
                sstore[name] = kb.sem("st_" + name)
            return sstore[name]

        def s0():
            def dma(dst_ap, src_ap):
                kb.do("sync", lambda e, dd=dst_ap, ss=src_ap: e.dma_start(out=dd, in_=ss),
                      reads=(), writes=(rK,), sem=ld("konst"), d=16)
            dma(k_dtbias[:, :], dtbias[:, :]); dma(k_acol[:, :], acol[:, :])
            dma(k_bthr[:, :], bthr[:, :]); dma(k_tri[:, :], tri[:, :])
            dma(k_mmat[:, :], mmat[:, :]); dma(k_identf[:, :], identf[:, :])
            dma(k_identb[:, :], identb[:, :]); dma(k_isodd[:, :], isodd[:, :])
            dma(k_onesb[:, :], onesb[:, :])
            dma(k_dtm[:, :], dtm[:, :])
            dma(k_onesrow[:, :], onesrow[:, :])
            dma(k_dexp[:, :], dexp[:, :]); dma(k_nwexp[:, :], nwexp[:, :])
            dma(k_bbase[:, :], bbase[:, :]); dma(k_convb[:, :], convb[:, :])
            dma(k_convw[:, :], convw[:, :])
            kb.cop("vector", lambda e: e.memset(k_eps[:, :], EPS), writes=(rK,))
            kb.cop("vector", lambda e: e.memset(k_one[:, :], 1.0), writes=(rK,))

        def mixer_front():
            with ExitStack() as sk:
                lb = alloc_ln_bufs(sk, "_mf")
                xtile, rxtile = lb["xtile"][0], lb["rxtile"][0]
                xsq, rxsq = lb["xsq"], lb["rxsq"]
                lnx = [sb(sk, "lnxA", [128, DC * W], BF16), sb(sk, "lnxB", [128, DC * W], BF16)]
                rlnx = [kb.res("lnxA"), kb.res("lnxB")]
                lnxh = sb(sk, "lnxh", [128, DC * 3], BF16); rlnxh = kb.res("lnxh")
                wslab = [sb(sk, f"wslab{i}", [128, DC * 128], BF16) for i in range(2)]
                rwslab = [kb.res(f"wslab{i}") for i in range(2)]
                stg = [sb(sk, f"stg{i}", [128, W], BF16) for i in range(4)]
                rstg = [kb.res(f"stg{i}") for i in range(4)]
                hstg = sb(sk, "hstg", [128, 4], BF16); rhstg = kb.res("hstg")
                dtraw_l = sb(sk, "dtraw", [128, T]); rdtraw = kb.res("dtraw")

                def ln_halo():
                    ap = _ap(xhalo.tensor, 0, [[3, 128], [128 * 3, DC], [1, 3]])
                    kb.do("sync", lambda e, s=ap: e.dma_start(
                        out=xtile[:, :DC * 3].rearrange("p (c t) -> p c t", c=DC), in_=s),
                        reads=(rD["xhalo"],), writes=(rxtile,), sem=ld(xtile.name), d=16)
                    kb.cop("scalar", lambda e: e.activation(
                        xsq[:, :DC * 3], xtile[:, :DC * 3], AF.Square),
                        reads=(rxtile,), writes=(rxsq,))
                    kb.cop("vector", lambda e: e.tensor_copy(
                        lnxh[:, :DC * 3], xtile[:, :DC * 3]),
                        reads=(rxtile,), writes=(rlnxh,))
                    for kc in range(DC):
                        kb.do("tensor", lambda e, k=kc: e.matmul(
                            pB[0][0:1, :3], k_onesb[:, 0:1], lnxh[:, k * 3:(k + 1) * 3],
                            start=(k == 0), stop=(k == DC - 1)),
                            reads=(rlnxh, rK), writes=(rpB[0],),
                            sem=(kb.ms["tensor"] if kc == DC - 1 else None))
                    for kc in range(DC):
                        kb.do("tensor", lambda e, k=kc: e.matmul(
                            pB[1][0:1, :3], k_onesb[:, 0:1], xsq[:, k * 3:(k + 1) * 3],
                            start=(k == 0), stop=(k == DC - 1)),
                            reads=(rxsq, rK), writes=(rpB[1],),
                            sem=(kb.ms["tensor"] if kc == DC - 1 else None))
                    nd = float(D)
                    kb.cop("vector", lambda e: e.tensor_scalar(
                        trow[:, :3], pB[0][0:1, :3], 1.0 / nd, None, op0=ALU.mult),
                        reads=(rpB[0],), writes=(rtrow,))
                    kb.cop("vector", lambda e: e.tensor_tensor(
                        trow2[:, :3], trow[:, :3], trow[:, :3], op=ALU.mult),
                        reads=(rtrow,), writes=(rtrow2,))
                    kb.cop("vector", lambda e: e.scalar_tensor_tensor(
                        trow2[:, :3], pB[1][0:1, :3], 1.0 / nd, trow2[:, :3],
                        op0=ALU.mult, op1=ALU.subtract),
                        reads=(rpB[1], rtrow2), writes=(rtrow2,))
                    kb.cop("scalar", lambda e: e.activation(
                        trow2[:, :3], trow2[:, :3], AF.Sqrt, bias=k_eps[0:1, 0:1]),
                        reads=(rtrow2, rK), writes=(rtrow2,))
                    kb.cop("vector", lambda e: e.reciprocal(trow2[:, :3], trow2[:, :3]),
                        reads=(rtrow2,), writes=(rtrow2,))
                    kb.cop("vector", lambda e: e.tensor_copy(trowb[:, :3], trow2[:, :3]),
                        reads=(rtrow2,), writes=(rtrowb,))
                    kb.cop("vector", lambda e: e.tensor_tensor(
                        trow[:, :3], trow[:, :3], trow2[:, :3], op=ALU.mult),
                        reads=(rtrow, rtrow2), writes=(rtrow,))
                    kb.cop("vector", lambda e: e.tensor_scalar(
                        trowb2[:, :3], trow[:, :3], -1.0, None, op0=ALU.mult),
                        reads=(rtrow,), writes=(rtrowb2,))
                    kb.cop("tensor", lambda e: e.matmul(
                        pB[0][:, :3], k_onesrow[:, :], trowb[:, :3], start=True, stop=True),
                        reads=(rtrowb, rK), writes=(rpB[0],))
                    kb.cop("tensor", lambda e: e.matmul(
                        pB[1][:, :3], k_onesrow[:, :], trowb2[:, :3], start=True, stop=True),
                        reads=(rtrowb2, rK), writes=(rpB[1],))
                    for kc in range(DC):
                        kb.cop("vector", lambda e, k=kc: e.tensor_tensor(
                            tchunk[:, :3], xtile[:, k * 3:(k + 1) * 3], pB[0][:, :3], op=ALU.mult),
                            reads=(rxtile, rpB[0]), writes=(rtchunk,))
                        kb.cop("vector", lambda e, k=kc: e.tensor_tensor(
                            lnxh[:, k * 3:(k + 1) * 3], tchunk[:, :3], pB[1][:, :3], op=ALU.add),
                            reads=(rtchunk, rpB[1]), writes=(rlnxh,))

                def inproj_pass(tiles, do_halo):
                    nt = len(tiles)
                    for o in range(POC):
                        sl, rsl = wslab[o % 2], rwslab[o % 2]
                        ap = _ap(win_t.tensor, o * 128, [[PROJ, 128], [128 * PROJ, DC], [1, 128]])
                        kb.do("gpsimd", lambda e, s=ap, dd=sl: e.dma_start(
                            out=dd[:].rearrange("p (c o) -> p c o", c=DC), in_=s),
                            reads=(), writes=(rsl,), sem=ld(f"wslab{o % 2}"), d=16)
                        is_gate = o < GC
                        is_dt = o == POC - 1
                        xbc_o = o - GC
                        for ti in range(nt):
                            lx, rlx = lnx[ti], rlnx[ti]
                            bi = (o * nt + ti) % 4
                            bank, rbank = pA[bi], rpA[bi]
                            for kc in range(DC):
                                kb.do("tensor", lambda e, k=kc, b=bank, s=sl, x=lx: e.matmul(
                                    b[:, :W], s[:, k * 128:(k + 1) * 128], x[:, k * W:(k + 1) * W],
                                    start=(k == 0), stop=(k == DC - 1)),
                                    reads=(rsl, rlx), writes=(rbank,),
                                    sem=(kb.ms["tensor"] if kc == DC - 1 else None))
                            if is_dt:
                                kb.cop("scalar", lambda e, b=bank, t0=tiles[ti] * W: e.activation(
                                    dtraw_l[:, t0:t0 + W], b[:, :W], AF.Identity),
                                    reads=(rbank,), writes=(rdtraw,))
                            else:
                                stb, rstb = stg[bi], rstg[bi]
                                fn = AF.Silu if is_gate else AF.Identity
                                kb.cop("scalar", lambda e, b=bank, s=stb, f=fn: e.activation(
                                    s[:, :W], b[:, :W], f),
                                    reads=(rbank,), writes=(rstb,))
                                if is_gate:
                                    dst = _ap(s_gate.tensor, o * 128 * T + tiles[ti] * W, [[T, 128], [1, W]])
                                    rdst = rD["s_gate"]
                                else:
                                    dst = _ap(s_xbc.tensor, xbc_o * 128 * TH + 3 + tiles[ti] * W, [[TH, 128], [1, W]])
                                    rdst = rD["s_xbc"]
                                kb.do("sync", lambda e, dd=dst, s=stb: e.dma_start(out=dd, in_=s[:, :W]),
                                      reads=(rstb,), writes=(rdst,), sem=sst(f"stg{bi}"), d=16)
                        if do_halo and (not is_gate) and (not is_dt):
                            for kc in range(DC):
                                kb.do("tensor", lambda e, k=kc, s=sl: e.matmul(
                                    pB[1][:, 512:515], s[:, k * 128:(k + 1) * 128], lnxh[:, k * 3:(k + 1) * 3],
                                    start=(k == 0), stop=(k == DC - 1)),
                                    reads=(rsl, rlnxh), writes=(rpB[1],),
                                    sem=(kb.ms["tensor"] if kc == DC - 1 else None))
                            kb.cop("scalar", lambda e: e.activation(
                                hstg[:, 0:3], pB[1][:, 512:515], AF.Identity),
                                reads=(rpB[1],), writes=(rhstg,))
                            dst = _ap(s_xbc.tensor, xbc_o * 128 * TH, [[TH, 128], [1, 3]])
                            kb.do("sync", lambda e, dd=dst: e.dma_start(out=dd, in_=hstg[:, 0:3]),
                                  reads=(rhstg,), writes=(rD["s_xbc"],), sem=sst("hstg"), d=16)

                def dt_softplus():
                    dts1 = sb(sk, "dts1", [128, T]); rdts1 = kb.res("dts1")
                    kb.cop("vector", lambda e: e.tensor_scalar(
                        dtraw_l[:, :T], dtraw_l[:, :T], k_dtbias[:, 0:1], None, op0=ALU.add),
                        reads=(rdtraw, rK), writes=(rdtraw,))
                    kb.cop("scalar", lambda e: e.activation(dts1[:, :T], dtraw_l[:, :T], AF.Abs),
                        reads=(rdtraw,), writes=(rdts1,))
                    kb.cop("scalar", lambda e: e.activation(dts1[:, :T], dts1[:, :T], AF.Exp, scale=-1.0),
                        reads=(rdts1,), writes=(rdts1,))
                    kb.cop("scalar", lambda e: e.activation(dts1[:, :T], dts1[:, :T], AF.Ln, bias=k_one[:, 0:1]),
                        reads=(rdts1, rK), writes=(rdts1,))
                    kb.cop("vector", lambda e: e.tensor_scalar(
                        dtraw_l[:, :T], dtraw_l[:, :T], 0.0, None, op0=ALU.max),
                        reads=(rdtraw,), writes=(rdtraw,))
                    kb.cop("vector", lambda e: e.tensor_tensor(
                        dtraw_l[:, :T], dtraw_l[:, :T], dts1[:, :T], op=ALU.add),
                        reads=(rdtraw, rdts1), writes=(rdtraw,))
                    kb.cop("vector", lambda e: e.tensor_scalar(
                        dtraw_l[:, :T], dtraw_l[:, :T], DT_MIN, DT_MAX, op0=ALU.max, op1=ALU.min),
                        reads=(rdtraw,), writes=(rdtraw,))
                    kb.do("sync", lambda e: e.dma_start(out=scr["s_dt"], in_=dtraw_l[:, :T]),
                          reads=(rdtraw,), writes=(rD["s_dt"],), sem=sst("dtraw"), d=16)

                def conv():
                    dgtile_ = [sb(sk, f"dgtile{i}", [128, K * 128], BF16) for i in range(2)]
                    rdgtile_ = [kb.res(f"dgtile{i}") for i in range(2)]
                    xbcw_ = [sb(sk, f"xbcw{i}", [128, W + 3], BF16) for i in range(4)]
                    rxbcw_ = [kb.res(f"xbcw{i}") for i in range(4)]
                    for ch in range(CC):
                        dgtile, rdgtile = dgtile_[ch % 2], rdgtile_[ch % 2]
                        for j in range(K):
                            kb.cop("vector", lambda e, jj=j, ch_=ch, dg=dgtile: e.tensor_scalar(
                                dg[:, jj * 128:(jj + 1) * 128], k_identb[:, :],
                                k_convw[:, ch_ * K + jj:ch_ * K + jj + 1], None, op0=ALU.mult),
                                reads=(rK,), writes=(rdgtile,))
                        for tj in range(NTILE):
                            src = _ap(s_xbc.tensor, ch * 128 * TH + tj * W, [[TH, 128], [1, W + 3]])
                            wbi = (ch * NTILE + tj) % 4
                            xbcw, rxbcw = xbcw_[wbi], rxbcw_[wbi]
                            kb.do("sync", lambda e, s=src, xw=xbcw: e.dma_start(out=xw[:, :W + 3], in_=s),
                                  reads=(rD["s_xbc"],), writes=(rxbcw,), sem=ld(f"xbcw{wbi}"), d=16)
                            bi = (ch * NTILE + tj) % 3
                            bank, rbank = pA[bi], rpA[bi]
                            for j in range(K):
                                kb.do("tensor", lambda e, jj=j, b=bank, dg=dgtile, xw=xbcw: e.matmul(
                                    b[:, :W], dg[:, jj * 128:(jj + 1) * 128], xw[:, jj:jj + W],
                                    start=(jj == 0), stop=(jj == K - 1)),
                                    reads=(rdgtile, rxbcw), writes=(rbank,),
                                    sem=(kb.ms["tensor"] if j == K - 1 else None))
                            stb, rstb = stg[(ch * NTILE + tj) % 4], rstg[(ch * NTILE + tj) % 4]
                            kb.cop("scalar", lambda e, b=bank, s=stb, ch_=ch: e.activation(
                                s[:, :W], b[:, :W], AF.Silu, bias=k_convb[:, ch_:ch_ + 1]),
                                reads=(rbank, rK), writes=(rstb,))
                            if ch < IC:
                                dst = _ap(s_xfm.tensor, ch * 128 * T + tj * W, [[T, 128], [1, W]])
                                kb.do("sync", lambda e, dd=dst, s=stb: e.dma_start(out=dd, in_=s[:, :W]),
                                      reads=(rstb,), writes=(rD["s_xfm"],), sem=sst(stb.name), d=16)
                            else:
                                if ch < IC + 8:
                                    dst = _ap(s_bfm.tensor, (ch - IC) * 128 * T + tj * W, [[T, 128], [1, W]])
                                    kb.do("sync", lambda e, dd=dst, s=stb: e.dma_start(out=dd, in_=s[:, :W]),
                                          reads=(rstb,), writes=(rD["s_bfm"],), sem=sst(stb.name), d=16)
                                c0 = tj * W // L
                                ncl = W // L
                                dst2 = _ap(s_bcf.tensor, c0 * 128 * 16 * L + (ch - IC) * L,
                                           [[16 * L, 128], [128 * 16 * L, ncl], [1, L]])
                                kb.do("sync", lambda e, dd=dst2, s=stb, n=ncl: e.dma_start(
                                    out=dd, in_=s[:, :W].rearrange("p (c l) -> p c l", c=n)),
                                    reads=(rstb,), writes=(rD["s_bcf"],), sem=sst("b" + stb.name), d=16)

                ln_halo()
                half = max(1, NTILE // 2)
                passes = [p for p in (list(range(0, half)), list(range(half, NTILE))) if p]
                for pi, ptiles in enumerate(passes):
                    for i, tj in enumerate(ptiles):
                        emit_ln(lb, xin, rD["xin"], T, tj * W, W, DC, "bf16", out_sb=lnx[i], rout=rlnx[i])
                    inproj_pass(ptiles, do_halo=(pi == 0))
                dt_softplus()
                conv()
            kb.barrier()

        def ssd_phase_a():
            with ExitStack() as sk:
                NB = 2
                zeros128 = sb(sk, "zeros128", [128, L]); rzeros = kb.res("zeros128")
                cumlog = sb(sk, "cumlog", [128, 1]); rcumlog = kb.res("cumlog")
                carry = sb(sk, "carry", [128, INTER]); rcarry = kb.res("carry")
                prevb = [sb(sk, f"prevb{i}", [128, INTER], BF16) for i in range(2)]
                rprevb = [kb.res(f"prevb{i}") for i in range(2)]
                x_tm = [sb(sk, f"x_tm{i}", [128, INTER], BF16) for i in range(2)]
                rx_tm = [kb.res(f"x_tm{i}") for i in range(2)]
                xddg = [sb(sk, f"xddg{i}", [128, 1024], BF16) for i in range(2)]
                rxddg = [kb.res(f"xddg{i}") for i in range(2)]
                dtTf = sb(sk, "dtTf", [128, L]); rdtTf = kb.res("dtTf")
                csT2 = [sb(sk, f"csT2_{i}", [128, L]) for i in range(2)]
                rcsT2 = [kb.res(f"csT2_{i}") for i in range(2)]
                b_tm = sb(sk, "b_tm", [128, NG * SS], BF16); rb_tm = kb.res("b_tm")
                ystg = [sb(sk, f"ystg{i}", [128, 512]) for i in range(2)]
                rystg = [kb.res(f"ystg{i}") for i in range(2)]
                dtc = [sb(sk, f"dtc{i}", [128, L]) for i in range(NB)]
                rdtc = [kb.res(f"dtc{i}") for i in range(NB)]
                dAc = [sb(sk, f"dAc{i}", [128, L]) for i in range(NB)]
                rdAc = [kb.res(f"dAc{i}") for i in range(NB)]
                csc = [sb(sk, f"csc{i}", [128, L]) for i in range(NB)]
                rcsc = [kb.res(f"csc{i}") for i in range(NB)]
                csTf = [sb(sk, f"csTf{i}", [128, L]) for i in range(NB)]
                rcsTf = [kb.res(f"csTf{i}") for i in range(NB)]
                gexpb = [sb(sk, f"gexpb{i}", [128, L], BF16) for i in range(NB)]
                rgexpb = [kb.res(f"gexpb{i}") for i in range(NB)]
                gexp2b = [sb(sk, f"gexp2b{i}", [128, L], BF16) for i in range(NB)]
                rgexp2b = [kb.res(f"gexp2b{i}") for i in range(NB)]
                gexptb = [sb(sk, f"gexptb{i}", [128, L], BF16) for i in range(NB)]
                rgexptb = [kb.res(f"gexptb{i}") for i in range(NB)]
                cdexp = [sb(sk, f"cdexp{i}", [128, L], BF16) for i in range(NB)]
                rcdexp = [kb.res(f"cdexp{i}") for i in range(NB)]
                dsf = [sb(sk, f"dsf{i}", [128, L]) for i in range(NB)]
                rdsf = [kb.res(f"dsf{i}") for i in range(NB)]
                ddTb = [sb(sk, f"ddTb{i}", [128, L], BF16) for i in range(NB)]
                rddTb = [kb.res(f"ddTb{i}") for i in range(NB)]
                bcw = [sb(sk, f"bcw{i}", [128, 16 * L], BF16) for i in range(NB)]
                rbcw = [kb.res(f"bcw{i}") for i in range(NB)]
                xDg = [sb(sk, f"xDg{i}", [128, 1024], BF16) for i in range(2)]
                rxDg = [kb.res(f"xDg{i}") for i in range(2)]
                csb = [sb(sk, f"csb{i}", [128, 16 * L]) for i in range(3)]
                rcsb = [kb.res(f"csb{i}") for i in range(3)]
                Eb = [sb(sk, f"Eb{i}", [128, 16 * L], BF16) for i in range(3)]
                rEb = [kb.res(f"Eb{i}") for i in range(3)]
                scT = [sb(sk, f"scT{i}", [128, 16 * L], BF16) for i in range(2)]
                rscT = [kb.res(f"scT{i}") for i in range(2)]
                e2b = [sb(sk, f"e2b{i}", [128, 16 * L], BF16) for i in range(3)]
                re2b = [kb.res(f"e2b{i}") for i in range(3)]
                cgb = [sb(sk, f"cgb{i}", [128, 16 * L], BF16) for i in range(2)]
                rcgb = [kb.res(f"cgb{i}") for i in range(2)]
                cbm = [sb(sk, f"cbm{i}", [128, L], BF16) for i in range(2)]
                rcbm = [kb.res(f"cbm{i}") for i in range(2)]

                kb.cop("vector", lambda e: e.memset(zeros128[:, :], 0.0), writes=(rzeros,))
                kb.cop("vector", lambda e: e.memset(cumlog[:, :], 0.0), writes=(rcumlog,))
                kb.cop("vector", lambda e: e.memset(carry[:, :], 0.0), writes=(rcarry,))

                for ci in range(NT):
                    t0 = ci * L
                    p = ci % NB
                    kb.do("sync", lambda e, t=t0, b=dtc[p]: e.dma_start(
                        out=b[:, :], in_=_ap(scr["s_dt"].tensor, t, [[T, 128], [1, L]])),
                        reads=(rD["s_dt"],), writes=(rdtc[p],), sem=ld(f"dtc{p}"), d=16)
                    kb.do("sync", lambda e, t=t0, b=x_tm[p]: e.dma_start(
                        out=b[:, :], in_=_ap(s_xfm.tensor, t, [[T, INTER], [1, L]]), transpose=True),
                        reads=(rD["s_xfm"],), writes=(rx_tm[p],), sem=ld(f"x_tm{p}"), d=16)
                    kb.do("sync", lambda e, t=t0, b=b_tm: e.dma_start(
                        out=b[:, :], in_=_ap(s_bfm.tensor, t, [[T, NG * SS], [1, L]]), transpose=True),
                        reads=(rD["s_bfm"],), writes=(rb_tm,), sem=ld("b_tm"), d=16)
                    kb.do("sync", lambda e, c=ci, b=bcw[p]: e.dma_start(
                        out=b[:, :], in_=_ap(s_bcf.tensor, c * 128 * 16 * L, [[16 * L, 128], [1, 16 * L]])),
                        reads=(rD["s_bcf"],), writes=(rbcw[p],), sem=ld(f"bcw{p}"), d=16)
                    kb.cop("vector", lambda e, b=dtc[p], o=dAc[p]: e.tensor_scalar(
                        o[:, :], b[:, :], k_acol[:, 0:1], None, op0=ALU.mult),
                        reads=(rdtc[p], rK), writes=(rdAc[p],))
                    kb.cop("vector", lambda e, i=p: e.tensor_tensor_scan(
                        csc[i][:, :], dAc[i][:, :], zeros128[:, :], 0.0, op0=ALU.add, op1=ALU.add),
                        reads=(rdAc[p], rzeros), writes=(rcsc[p],))
                    kb.do("sync", lambda e, c=ci, i=p: e.dma_start(
                        out=_ap(s_cs.tensor, c * NH * L, [[L, 128], [1, L]]), in_=csc[i][:, :]),
                        reads=(rcsc[p],), writes=(rD["s_cs"],), sem=sst(f"csc{p}"), d=16)
                    kb.cop("scalar", lambda e, i=p: e.activation(gexpb[i][:, :], csc[i][:, :], AF.Exp),
                        reads=(rcsc[p],), writes=(rgexpb[p],))
                    kb.do("sync", lambda e, c=ci, i=p: e.dma_start(
                        out=_ap(s_gexp.tensor, c * NH * L, [[L, 128], [1, L]]), in_=gexpb[i][:, :]),
                        reads=(rgexpb[p],), writes=(rD["s_gexp"],), sem=sst(f"gexpb{p}"), d=16)
                    kb.cop("scalar", lambda e, i=p: e.activation(
                        gexp2b[i][:, :], csc[i][:, :], AF.Exp, bias=cumlog[:, 0:1]),
                        reads=(rcsc[p], rcumlog), writes=(rgexp2b[p],))
                    kb.do("sync", lambda e, c=ci, i=p: e.dma_start(
                        out=_ap(s_gexp2.tensor, c * NH * L, [[L, 128], [1, L]]), in_=gexp2b[i][:, :]),
                        reads=(rgexp2b[p],), writes=(rD["s_gexp2"],), sem=sst(f"gexp2b{p}"), d=16)
                    kb.cop("vector", lambda e, i=p: e.tensor_tensor(
                        cumlog[:, 0:1], cumlog[:, 0:1], csc[i][:, L - 1:L], op=ALU.add),
                        reads=(rcumlog, rcsc[p]), writes=(rcumlog,))
                    kb.cop("tensor", lambda e, i=p: e.transpose(pA[0][:, 0:L], csc[i][:, :], k_identf[:, :]),
                        reads=(rcsc[p], rK), writes=(rpA[0],))
                    kb.cop("vector", lambda e, i=p: e.tensor_copy(csTf[i][:, :], pA[0][:, 0:L]),
                        reads=(rpA[0],), writes=(rcsTf[p],))
                    kb.cop("scalar", lambda e, i=p: e.activation(gexptb[i][:, :], csTf[i][:, :], AF.Exp),
                        reads=(rcsTf[p],), writes=(rgexptb[p],))
                    kb.do("sync", lambda e, t=t0, i=p: e.dma_start(
                        out=_ap(s_gexpt.tensor, t * NH, [[NH, 128], [1, NH]]), in_=gexptb[i][:, :]),
                        reads=(rgexptb[p],), writes=(rD["s_gexpt"],), sem=sst(f"gexptb{p}"), d=16)
                    kb.do("sync", lambda e, t=t0, i=p: e.dma_start(
                        out=cdexp[i][:, :], in_=_ap(s_gexpt.tensor, (t + L - 1) * NH, [[0, 128], [1, NH]])),
                        reads=(rD["s_gexpt"],), writes=(rcdexp[p],), sem=ld(f"cdexp{p}"), d=16)
                    kb.cop("scalar", lambda e, i=p: e.activation(
                        dsf[i][:, :], csc[i][:, :], AF.Exp, scale=-1.0, bias=csc[i][:, L - 1:L]),
                        reads=(rcsc[p],), writes=(rdsf[p],))
                    kb.cop("vector", lambda e, i=p: e.tensor_tensor(
                        dsf[i][:, :], dtc[i][:, :], dsf[i][:, :], op=ALU.mult),
                        reads=(rdtc[p], rdsf[p]), writes=(rdsf[p],))
                    kb.cop("tensor", lambda e, i=p: e.transpose(pA[0][:, 128:256], dsf[i][:, :], k_identf[:, :]),
                        reads=(rdsf[p], rK), writes=(rpA[0],))
                    kb.cop("vector", lambda e, i=p: e.tensor_copy(ddTb[i][:, :], pA[0][:, 128:256]),
                        reads=(rpA[0],), writes=(rddTb[p],))
                    kb.cop("tensor", lambda e, i=p: e.transpose(pA[0][:, 256:384], dtc[i][:, :], k_identf[:, :]),
                        reads=(rdtc[p], rK), writes=(rpA[0],))
                    kb.cop("vector", lambda e: e.tensor_copy(dtTf[:, :], pA[0][:, 256:384]),
                        reads=(rpA[0],), writes=(rdtTf,))
                    kb.cop("scalar", lambda e, i=p: e.activation(csT2[i][:, :], dtTf[:, :], AF.Ln),
                        reads=(rdtTf,), writes=(rcsT2[p],))
                    kb.cop("vector", lambda e, i=p: e.tensor_tensor(
                        csT2[i][:, :], csTf[i][:, :], csT2[i][:, :], op=ALU.subtract),
                        reads=(rcsTf[p], rcsT2[p]), writes=(rcsT2[p],))
                    for g in range(NG):
                        q3 = g % 2
                        rep = _ap(ddTb[p], g * 16, [[L, 128], [1, 16], [0, 64]])
                        kb.cop("vector", lambda e, gg=g, r=rep, xo=xddg[q3], xi=x_tm[p]: e.tensor_tensor(
                            xo[:].rearrange("p (h q) -> p h q", h=16),
                            xi[:, gg * 1024:(gg + 1) * 1024].rearrange("p (h q) -> p h q", h=16),
                            r, op=ALU.mult),
                            reads=(rx_tm[p], rddTb[p]), writes=(rxddg[q3],))
                        stp, rstp = pB[g % 2], rpB[g % 2]
                        kb.do("tensor", lambda e, gg=g, b=stp, bt=b_tm, xd=xddg[q3]: e.matmul(
                            b[:, 0:512], bt[:, gg * 128:(gg + 1) * 128],
                            xd[:, 0:512], start=True, stop=True),
                            reads=(rb_tm, rxddg[q3]), writes=(rstp,), sem=None)
                        kb.do("tensor", lambda e, gg=g, b=stp, bt=b_tm, xd=xddg[q3]: e.matmul(
                            b[:, 512:1024], bt[:, gg * 128:(gg + 1) * 128],
                            xd[:, 512:1024], start=True, stop=True),
                            reads=(rb_tm, rxddg[q3]), writes=(rstp,), sem=kb.ms["tensor"])
                        kb.cop("vector", lambda e, gg=g, pv=prevb[p]: e.tensor_copy(
                            pv[:, gg * 1024:(gg + 1) * 1024], carry[:, gg * 1024:(gg + 1) * 1024]),
                            reads=(rcarry,), writes=(rprevb[p],))
                        repcd = _ap(cdexp[p], g * 16, [[L, 128], [1, 16], [0, 64]])
                        kb.cop("vector", lambda e, gg=g, r=repcd: e.tensor_tensor(
                            carry[:, gg * 1024:(gg + 1) * 1024].rearrange("p (h q) -> p h q", h=16),
                            carry[:, gg * 1024:(gg + 1) * 1024].rearrange("p (h q) -> p h q", h=16),
                            r, op=ALU.mult),
                            reads=(rcarry, rcdexp[p]), writes=(rcarry,))
                        kb.cop("vector", lambda e, gg=g, b=stp: e.tensor_tensor(
                            carry[:, gg * 1024:(gg + 1) * 1024],
                            carry[:, gg * 1024:(gg + 1) * 1024], b[:, :], op=ALU.add),
                            reads=(rcarry, rstp), writes=(rcarry,))
                    def seg_chain(g):
                        q = g % 3
                        kb.do("sync", lambda e, gg=g, c=ci, b=csb[q]: e.dma_start(
                            out=b[:, :],
                            in_=_ap(s_cs.tensor, c * NH * L + gg * 16 * L, [[0, 128], [1, 16 * L]])),
                            reads=(rD["s_cs"],), writes=(rcsb[q],), sem=ld(f"csb{q}"), d=16)
                        repct = _ap(csT2[p], g * 16, [[L, 128], [1, 16], [0, 128]])
                        kb.cop("gpsimd", lambda e, b=csb[q], r=repct: e.tensor_tensor(
                            b[:].rearrange("p (h q2) -> p h q2", h=16),
                            b[:].rearrange("p (h q2) -> p h q2", h=16), r, op=ALU.subtract),
                            reads=(rcsb[q], rcsT2[p]), writes=(rcsb[q],))
                        kb.cop("vector", lambda e, b=csb[q]: e.tensor_scalar(
                            b[:, :], b[:, :], 4.70, None, op0=ALU.min),
                            reads=(rcsb[q],), writes=(rcsb[q],))
                        kb.cop("scalar", lambda e, b=csb[q], o=Eb[q]: e.activation(o[:, :], b[:, :], AF.Exp),
                            reads=(rcsb[q],), writes=(rEb[q],))
                        kb.do("sync", lambda e, gg=g, c=ci, b=e2b[q]: e.dma_start(
                            out=b[:, :],
                            in_=_ap(s_gexp.tensor, c * NH * L + gg * 16 * L, [[0, 128], [1, 16 * L]])),
                            reads=(rD["s_gexp"],), writes=(re2b[q],), sem=ld(f"e2b{q}"), d=16)
                    seg_chain(0)
                    seg_chain(1)
                    for g in range(NG):
                        q = g % 3
                        if g + 2 < NG:
                            seg_chain(g + 2)
                        q2 = g % 2
                        cb_bank, rcb_bank = pB[q2], rpB[q2]
                        kb.cop("tensor", lambda e, gg=g, b=cb_bank, i=p: e.matmul(
                            b[:, 0:L], bcw[i][:, gg * L:(gg + 1) * L],
                            bcw[i][:, 8 * L + gg * L:8 * L + (gg + 1) * L],
                            start=True, stop=True),
                            reads=(rbcw[p],), writes=(rcb_bank,))
                        kb.cop("vector", lambda e, b=cb_bank, o=cbm[q2]: e.tensor_tensor(
                            o[:, :], b[:, 0:L], k_tri[:, :], op=ALU.mult),
                            reads=(rcb_bank, rK), writes=(rcbm[q2],))
                        repcb = _ap(cbm[q2], 0, [[L, 128], [0, 16], [1, L]])
                        kb.cop("vector", lambda e, r=repcb, i=q2, i3=q: e.tensor_tensor(
                            scT[i][:].rearrange("p (h q2) -> p h q2", h=16), r,
                            Eb[i3][:].rearrange("p (h q2) -> p h q2", h=16), op=ALU.mult),
                            reads=(rcbm[q2], rEb[q]), writes=(rscT[q2],))
                        repc = _ap(bcw[p], 8 * L + g * L, [[16 * L, 128], [0, 16], [1, L]])
                        kb.cop("vector", lambda e, r=repc, i=q2, i3=q: e.tensor_tensor(
                            cgb[i][:].rearrange("p (h q2) -> p h q2", h=16), r,
                            e2b[i3][:].rearrange("p (h q2) -> p h q2", h=16), op=ALU.mult),
                            reads=(rbcw[p], re2b[q]), writes=(rcgb[q2],))
                        repDg = _ap(k_dtm, g * 16, [[128, 128], [1, 16], [0, 64]])
                        kb.cop("vector", lambda e, gg=g, r=repDg, xo=xDg[q2], xi=x_tm[p]: e.tensor_tensor(
                            xo[:].rearrange("p (h q2) -> p h q2", h=16),
                            xi[:, gg * 1024:(gg + 1) * 1024].rearrange("p (h q2) -> p h q2", h=16),
                            r, op=ALU.mult),
                            reads=(rx_tm[p], rK), writes=(rxDg[q2],))
                        for half in range(2):
                            yb = pA[2 + half]
                            ryb = rpA[2 + half]
                            for jp4 in range(4):
                                jp = half * 4 + jp4
                                j = g * 8 + jp
                                for hh in (0, 1):
                                    h = 2 * j + hh
                                    hl = h - g * 16
                                    kb.do("tensor", lambda e, h_=h, hl_=hl, o=64 * hh, b=yb,
                                          c4=jp4 * L, i=q2, xi=x_tm[p]: e.matmul(
                                        b[o:o + 64, c4:c4 + L],
                                        xi[:, h_ * 64:(h_ + 1) * 64],
                                        scT[i][:, hl_ * L:(hl_ + 1) * L],
                                        start=True, stop=False, tile_position=(0, o),
                                        skip_group_check=True),
                                        reads=(rx_tm[p], rscT[q2]), writes=(ryb,), sem=None)
                                for hh in (0, 1):
                                    h = 2 * j + hh
                                    hl = h - g * 16
                                    kb.do("tensor", lambda e, h_=h, hl_=hl, o=64 * hh, b=yb,
                                          c4=jp4 * L, i=q2, pv=prevb[p]: e.matmul(
                                        b[o:o + 64, c4:c4 + L],
                                        pv[:, h_ * 64:(h_ + 1) * 64],
                                        cgb[i][:, hl_ * L:(hl_ + 1) * L],
                                        start=False, stop=False, tile_position=(0, o),
                                        skip_group_check=True),
                                        reads=(rprevb[p], rcgb[q2]), writes=(ryb,), sem=None)
                                for hh in (0, 1):
                                    h = 2 * j + hh
                                    hl = h - g * 16
                                    kb.do("tensor", lambda e, hl_=hl, o=64 * hh, b=yb,
                                          c4=jp4 * L, xd=xDg[q2]: e.matmul(
                                        b[o:o + 64, c4:c4 + L],
                                        xd[:, hl_ * 64:(hl_ + 1) * 64],
                                        k_identb[:, :],
                                        start=False, stop=True, tile_position=(0, o),
                                        skip_group_check=True),
                                        reads=(rxDg[q2], rK), writes=(ryb,),
                                        sem=(kb.ms["tensor"] if (hh == 1 and jp4 == 3) else None))
                            j0 = g * 8 + half * 4
                            ys, rys = ystg[half], rystg[half]
                            kb.cop("vector", lambda e, b=yb, s=ys: e.tensor_copy(s[:, :], b[:, :]),
                                reads=(ryb,), writes=(rys,))
                            dst = _ap(s_ypart.tensor, t0 // L * 128 * IC * L + j0 * L,
                                      [[IC * L, 128], [1, 512]])
                            kb.do("sync", lambda e, dd=dst, s=ys: e.dma_start(out=dd, in_=s[:, :]),
                                reads=(rys,), writes=(rD["s_ypart"],), sem=sst(f"ystg{half}"), d=16)
                kb.do("sync", lambda e: e.dma_start(out=scr["s_carry"], in_=carry[:, :]),
                      reads=(rcarry,), writes=(rD["s_carry"],), sem=sst("carry"), d=16)
            kb.barrier()

        sdram("s_dt", [128, T], F32)
        rD["s_dt"] = kb.res("d_s_dt")
        sdram("s_ln2", [D, T], BF16)
        rD["s_ln2"] = kb.res("d_s_ln2")

        def carry_exchange(carrymb, rcarrymb, sk):
            carrym = sb(sk, "carrym", [128, INTER]); rcarrym = kb.res("carrym")
            if n_pairs > 0:
                groups = [[2 * i, 2 * i + 1] for i in range(n_pairs)]
                kb.do("gpsimd", lambda e, g=groups: e.collective_compute(
                    "AllGather", ALU.bypass, replica_groups=g,
                    ins=[scr["s_carry"]], outs=[scr["s_carryag"]]),
                    reads=(rD["s_carry"],), writes=(rD["s_carryag"],),
                    sem=kb.sem("cc"), d=1)
                kb.do("sync", lambda e: e.dma_start(
                    out=carrym[:, :], in_=_ap(scr["s_carryag"].tensor, 0, [[INTER, 128], [1, INTER]])),
                    reads=(rD["s_carryag"],), writes=(rcarrym,), sem=ld("carrym"), d=16)
                kb.cop("vector", lambda e: e.tensor_scalar(
                    carrym[:, :], carrym[:, :], k_isodd[:, 0:1], None, op0=ALU.mult),
                    reads=(rcarrym, rK), writes=(rcarrym,))
            else:
                kb.cop("vector", lambda e: e.memset(carrym[:, :], 0.0), writes=(rcarrym,))
            kb.cop("vector", lambda e: e.tensor_copy(carrymb[:, :], carrym[:, :]),
                reads=(rcarrym,), writes=(rcarrymb,))

        def phase_b_gating():
            with ExitStack() as sk:
                carrymb = sb(sk, "carrymb", [128, INTER], BF16); rcarrymb = kb.res("carrymb")
                with ExitStack() as sk2:
                    carry_exchange(carrymb, rcarrymb, sk2)
                    kb.barrier()
                ybuf = [sb(sk, f"ybuf2_{i}", [128, IC * L]) for i in range(2)]
                rybuf = [kb.res(f"ybuf2_{i}") for i in range(2)]
                gatec_ = [sb(sk, f"gatec{i}", [128, IC * L], BF16) for i in range(2)]
                rgatec_ = [kb.res(f"gatec{i}") for i in range(2)]
                bcwC = [sb(sk, f"bcwC2_{i}", [128, 8 * L], BF16) for i in range(2)]
                rbcwC = [kb.res(f"bcwC2_{i}") for i in range(2)]
                e2b = [sb(sk, f"e2b2_{i}", [128, 16 * L], BF16) for i in range(3)]
                re2b = [kb.res(f"e2b2_{i}") for i in range(3)]
                cgb = [sb(sk, f"cgb2_{i}", [128, 16 * L], BF16) for i in range(2)]
                rcgb = [kb.res(f"cgb2_{i}") for i in range(2)]
                ysq = sb(sk, "ysq", [128, IC * L], BF16); rysq = kb.res("ysq")
                nrm = sb(sk, "nrm", [128, IC * L], BF16); rnrm = kb.res("nrm")
                ssrow = sb(sk, "ssrow", [1, 8 * L]); rssrow = kb.res("ssrow")
                ssrb = sb(sk, "ssrb", [1, 8 * L], BF16); rssrb = kb.res("ssrb")

                for ci in range(NT):
                    t0 = ci * L
                    p = ci % 2
                    yb_, ryb_ = ybuf[p], rybuf[p]
                    kb.do("sync", lambda e, c=ci, b=yb_: e.dma_start(
                        out=b[:, :], in_=_ap(s_ypart.tensor, c * 128 * IC * L, [[IC * L, 128], [1, IC * L]])),
                        reads=(rD["s_ypart"],), writes=(ryb_,), sem=ld(f"ybuf2_{p}"), d=16)
                    gatec, rgatec = gatec_[p], rgatec_[p]
                    kb.do("sync", lambda e, t=t0, gc=gatec: e.dma_start(
                        out=gc[:].rearrange("p (c t) -> p c t", c=IC),
                        in_=_ap(s_gate.tensor, t, [[T, 128], [128 * T, IC], [1, L]])),
                        reads=(rD["s_gate"],), writes=(rgatec,), sem=ld(f"gatec{p}"), d=16)
                    kb.do("sync", lambda e, c=ci, b=bcwC[p]: e.dma_start(
                        out=b[:, :], in_=_ap(s_bcf.tensor, c * 128 * 16 * L + 8 * L, [[16 * L, 128], [1, 8 * L]])),
                        reads=(rD["s_bcf"],), writes=(rbcwC[p],), sem=ld(f"bcwC2_{p}"), d=16)
                    def cg_chain(g):
                        q3 = g % 3
                        kb.do("sync", lambda e, gg=g, c=ci, b=e2b[q3]: e.dma_start(
                            out=b[:, :],
                            in_=_ap(s_gexp2.tensor, c * NH * L + gg * 16 * L, [[0, 128], [1, 16 * L]])),
                            reads=(rD["s_gexp2"],), writes=(re2b[q3],), sem=ld(f"e2b2_{q3}"), d=16)
                        repc = _ap(bcwC[p], g * L, [[8 * L, 128], [0, 16], [1, L]])
                        kb.cop("gpsimd", lambda e, r=repc, i=g % 2, i3=q3: e.tensor_tensor(
                            cgb[i][:].rearrange("p (h q2) -> p h q2", h=16), r,
                            e2b[i3][:].rearrange("p (h q2) -> p h q2", h=16), op=ALU.mult),
                            reads=(rbcwC[p], re2b[q3]), writes=(rcgb[i := g % 2] if False else rcgb[g % 2],))
                    cg_chain(0)
                    cg_chain(1)
                    for g in range(NG):
                        q = g % 2
                        if g + 2 < NG:
                            cg_chain(g + 2)
                        for half in range(2):
                            pbk = pA[1 + half]
                            rpbk = rpA[1 + half]
                            for jp4 in range(4):
                                jp = half * 4 + jp4
                                j = g * 8 + jp
                                for hh in (0, 1):
                                    h = 2 * j + hh
                                    hl = h - g * 16
                                    kb.do("tensor", lambda e, h_=h, hl_=hl, o=64 * hh, b=pbk,
                                          c4=jp4 * L, i=q: e.matmul(
                                        b[o:o + 64, c4:c4 + L],
                                        carrymb[:, h_ * 64:(h_ + 1) * 64],
                                        cgb[i][:, hl_ * L:(hl_ + 1) * L],
                                        start=True, stop=True, tile_position=(0, o),
                                        skip_group_check=True),
                                        reads=(rcarrymb, rcgb[q]), writes=(rpbk,),
                                        sem=(kb.ms["tensor"] if (hh == 1 and jp4 == 3) else None))
                            j0 = g * 8 + half * 4
                            kb.cop("vector", lambda e, b=pbk, y=yb_, j0_=j0: e.tensor_tensor(
                                y[:, j0_ * L:(j0_ + 4) * L], y[:, j0_ * L:(j0_ + 4) * L],
                                b[:, :], op=ALU.add),
                                reads=(ryb_, rpbk), writes=(ryb_,))
                    kb.cop("vector", lambda e, y=yb_, gc=gatec: e.tensor_tensor(
                        y[:, :], y[:, :], gc[:, :], op=ALU.mult),
                        reads=(ryb_, rgatec), writes=(ryb_,))
                    kb.cop("scalar", lambda e, y=yb_: e.activation(ysq[:, :], y[:, :], AF.Square),
                        reads=(ryb_,), writes=(rysq,))
                    for g in range(NG):
                        for jj in range(8):
                            kb.do("tensor", lambda e, gg=g, j2=jj: e.matmul(
                                pB[0][0:1, gg * L:(gg + 1) * L], k_onesb[:, 0:1],
                                ysq[:, (gg * 8 + j2) * L:(gg * 8 + j2 + 1) * L],
                                start=(j2 == 0), stop=(j2 == 7), skip_group_check=True),
                                reads=(rysq, rK), writes=(rpB[0],),
                                sem=(kb.ms["tensor"] if jj == 7 else None))
                    kb.cop("vector", lambda e: e.tensor_scalar(
                        ssrow[:, :], pB[0][0:1, 0:8 * L], 1.0 / 1024.0, None, op0=ALU.mult),
                        reads=(rpB[0],), writes=(rssrow,))
                    kb.cop("scalar", lambda e: e.activation(
                        ssrow[:, :], ssrow[:, :], AF.Sqrt, bias=k_eps[0:1, 0:1]),
                        reads=(rssrow, rK), writes=(rssrow,))
                    kb.cop("vector", lambda e: e.reciprocal(ssrow[:, :], ssrow[:, :]),
                        reads=(rssrow,), writes=(rssrow,))
                    kb.cop("vector", lambda e: e.tensor_copy(ssrb[:, :], ssrow[:, :]),
                        reads=(rssrow,), writes=(rssrb,))
                    for g in range(NG):
                        kb.cop("tensor", lambda e, gg=g: e.matmul(
                            pB[1][:, gg * L:(gg + 1) * L], k_onesrow[:, :],
                            ssrb[0:1, gg * L:(gg + 1) * L], start=True, stop=True,
                            skip_group_check=True),
                            reads=(rssrb, rK), writes=(rpB[1],))
                    for g in range(NG):
                        repnw = _ap(k_nwexp, 8 * g, [[IC, 128], [1, 8], [0, L]])
                        kb.cop("vector", lambda e, gg=g, r=repnw, y=yb_: e.tensor_tensor(
                            y[:, gg * 1024:(gg + 1) * 1024].rearrange("p (c t) -> p c t", c=8),
                            y[:, gg * 1024:(gg + 1) * 1024].rearrange("p (c t) -> p c t", c=8),
                            r, op=ALU.mult),
                            reads=(ryb_, rK), writes=(ryb_,))
                        reprs = _ap(pB[1], g * L, [[1024, 128], [0, 8], [1, L]])
                        kb.cop("vector", lambda e, gg=g, r=reprs, y=yb_: e.tensor_tensor(
                            nrm[:, gg * 1024:(gg + 1) * 1024].rearrange("p (c t) -> p c t", c=8),
                            y[:, gg * 1024:(gg + 1) * 1024].rearrange("p (c t) -> p c t", c=8),
                            r, op=ALU.mult),
                            reads=(ryb_, rpB[1]), writes=(rnrm,))
                    kb.do("sync", lambda e, t=t0: e.dma_start(
                        out=_ap(s_normed.tensor, t, [[T, 128], [128 * T, IC], [1, L]]),
                        in_=nrm[:].rearrange("p (c t) -> p c t", c=IC)),
                        reads=(rnrm,), writes=(rD["s_normed"],), sem=sst("nrm"), d=16)
            kb.barrier()

        def emit_mm_stage(rhs_dram, rrhs, rhs_nch, wsrc, noc, out_cb, tiles_per_pass=2,
                          wsrc_is_bf16=False, tag="mm"):
            with ExitStack() as sk:
                slab = [sb(sk, f"{tag}rs{i}", [128, rhs_nch * W], BF16) for i in range(tiles_per_pass)]
                rslab = [kb.res(f"{tag}rs{i}") for i in range(tiles_per_pass)]
                wsl = [sb(sk, f"{tag}ws{i}", [128, rhs_nch * 128], BF16) for i in range(2)]
                rwsl = [kb.res(f"{tag}ws{i}") for i in range(2)]
                npass = (NTILE + tiles_per_pass - 1) // tiles_per_pass
                KT = rhs_nch * 128
                for pi in range(npass):
                    tiles = [t for t in range(pi * tiles_per_pass, min((pi + 1) * tiles_per_pass, NTILE))]
                    for i, tj in enumerate(tiles):
                        ap = _ap(rhs_dram.tensor, tj * W, [[T, 128], [128 * T, rhs_nch], [1, W]])
                        kb.do("sync", lambda e, s=ap, d2=slab[i]: e.dma_start(
                            out=d2[:].rearrange("p (c t) -> p c t", c=rhs_nch), in_=s),
                            reads=(rrhs,), writes=(rslab[i],), sem=ld(f"{tag}rs{i}"), d=16)
                    for o in range(noc):
                        sl, rsl = wsl[o % 2], rwsl[o % 2]
                        ap = _ap(wsrc.tensor, o * 128, [[noc * 128, 128], [128 * noc * 128, rhs_nch], [1, 128]])
                        eng = "sync" if wsrc_is_bf16 else "gpsimd"
                        kb.do(eng, lambda e, s=ap, d2=sl: e.dma_start(
                            out=d2[:].rearrange("p (c o) -> p c o", c=rhs_nch), in_=s),
                            reads=(), writes=(rsl,), sem=ld(f"{tag}ws{o % 2}"), d=16)
                        for i, tj in enumerate(tiles):
                            bi = (o * len(tiles) + i) % 3
                            bank, rbank = pA[bi], rpA[bi]
                            for kc in range(rhs_nch):
                                kb.do("tensor", lambda e, k=kc, b=bank, s=sl, x=slab[i], n=rhs_nch: e.matmul(
                                    b[:, :W], s[:, k * 128:(k + 1) * 128], x[:, k * W:(k + 1) * W],
                                    start=(k == 0), stop=(k == n - 1)),
                                    reads=(rsl, rslab[i]), writes=(rbank,),
                                    sem=(kb.ms["tensor"] if kc == rhs_nch - 1 else None))
                            out_cb(o, tj, bank, rbank, sk)

        def out_proj_stage():
            stg8 = {}
            def s8_evac(o, tj, bank, rbank, sk):
                if "b" not in stg8:
                    stg8["b"] = [sb(sk, f"stg8{i}", [128, W], BF16) for i in range(3)]
                    stg8["r"] = [kb.res(f"stg8{i}") for i in range(3)]
                    stg8["f"] = [sb(sk, f"stg8f{i}", [128, W]) for i in range(3)]
                    stg8["rf"] = [kb.res(f"stg8f{i}") for i in range(3)]
                si = (o * NTILE + tj) % 3
                fb, rfb = stg8["f"][si], stg8["rf"][si]
                kb.cop("scalar", lambda e, b=bank, s=fb: e.activation(
                    s[:, :W], b[:, :W], AF.Identity),
                    reads=(rbank,), writes=(rfb,))
                dst = _ap(s_att.tensor, o * 128 * T + tj * W, [[T, 128], [1, W]])
                kb.do("sync", lambda e, dd=dst, s=fb: e.dma_start(out=dd, in_=s[:, :W]),
                      reads=(rfb,), writes=(rD["s_att"],), sem=sst(f"stg8f{si}"), d=16)
            emit_mm_stage(s_normed, rD["s_normed"], IC, wout_t, DC, s8_evac, tiles_per_pass=2, tag="op")
            kb.barrier()
            with ExitStack() as sk:
                lb = alloc_ln_bufs(sk, "_s8", with_out=True)
                for tj in range(NTILE):
                    emit_ln(lb, s_att, rD["s_att"], T, tj * W, W, DC, "resid",
                            resid_dram=(xin, rD["xin"]), dst_dram=s_x2, rdst=rD["s_x2"])
            kb.barrier()

        def ffn_stage():
            with ExitStack() as sk:
                lb = alloc_ln_bufs(sk, "_s9")
                ln2sb = sb(sk, "ln2sb", [128, DC * W], BF16); rln2sb = kb.res("ln2sb")
                for tj in range(NTILE):
                    emit_ln(lb, s_x2, rD["s_x2"], T, tj * W, W, DC, "bf16",
                            out_sb=ln2sb, rout=rln2sb)
                    dst = _ap(scr["s_ln2"].tensor, tj * W, [[T, 128], [128 * T, DC], [1, W]])
                    kb.do("sync", lambda e, dd=dst: e.dma_start(
                        out=dd, in_=ln2sb[:].rearrange("p (c t) -> p c t", c=DC)),
                        reads=(rln2sb,), writes=(rD["s_ln2"],), sem=sst("ln2sb"), d=16)
            kb.barrier()
            with ExitStack() as sk:
                k_bstack = sb(sk, "k_bstack", [128, D], BF16)
                rbstack = kb.res("k_bstack")
                kb.do("gpsimd", lambda e: e.dma_start(out=k_bstack[:, :], in_=bstack[:, :]),
                      reads=(), writes=(rbstack,), sem=ld("k_bstack"), d=16)
                slab = sb(sk, "f9rs", [128, DC * W], BF16); rslab = kb.res("f9rs")
                wsl = [sb(sk, f"f9ws{i}", [128, DC * 128], BF16) for i in range(2)]
                rwsl = [kb.res(f"f9ws{i}") for i in range(2)]
                hbuf = sb(sk, "hbuf", [128, W], BF16); rhbuf = kb.res("hbuf")
                hwb = sb(sk, "hwb", [128, W], BF16); rhwb = kb.res("hwb")
                glb = sb(sk, "glb", [128, W]); rglb = kb.res("glb")
                gt = sb(sk, "gt", [128, 128]); rgt = kb.res("gt")
                e8 = sb(sk, "e8", [128, 8]); re8 = kb.res("e8")
                gcol = sb(sk, "gcol", [128, 4]); rgcol = kb.res("gcol")
                wn = sb(sk, "wn", [128, 128]); rwn = kb.res("wn")

                wfmb = sb(sk, "wfmb", [8, 128], BF16); rwfmb = kb.res("wfmb")
                stgf = [sb(sk, f"stg9f{i}", [128, W]) for i in range(3)]
                rstgf = [kb.res(f"stg9f{i}") for i in range(3)]
                kb.cop("vector", lambda e: e.memset(wn[:, :], 0.0), writes=(rwn,))
                order = [32, 33] + list(range(32))
                for tj in range(NTILE):
                    ap = _ap(scr["s_ln2"].tensor, tj * W, [[T, 128], [128 * T, DC], [1, W]])
                    kb.do("sync", lambda e, s=ap: e.dma_start(
                        out=slab[:].rearrange("p (c t) -> p c t", c=DC), in_=s),
                        reads=(rD["s_ln2"],), writes=(rslab,), sem=ld("f9rs"), d=16)
                    for oi, o in enumerate(order):
                        sl, rsl = wsl[oi % 2], rwsl[oi % 2]
                        ap = _ap(wffn1_t.tensor, o * 128, [[FFN1, 128], [128 * FFN1, DC], [1, 128]])
                        kb.do("gpsimd", lambda e, s=ap, d2=sl: e.dma_start(
                            out=d2[:].rearrange("p (c o) -> p c o", c=DC), in_=s),
                            reads=(), writes=(rsl,), sem=ld(f"f9ws{oi % 2}"), d=16)
                        bank, rbank = pA[oi % 2], rpA[oi % 2]
                        for kc in range(DC):
                            last = (kc == DC - 1) and (o >= 32)
                            kb.do("tensor", lambda e, k=kc, b=bank, s=sl, lt=last: e.matmul(
                                b[:, :W], s[:, k * 128:(k + 1) * 128], slab[:, k * W:(k + 1) * W],
                                start=(k == 0), stop=lt),
                                reads=(rsl, rslab), writes=(rbank,),
                                sem=(kb.ms["tensor"] if last else None))
                        if o < 32:
                            kb.do("tensor", lambda e, b=bank, o_=o: e.matmul(
                                b[:, :W], k_bstack[:, o_ * 128:(o_ + 1) * 128], hwb[:, :W],
                                start=False, stop=True),
                                reads=(rbstack, rhwb), writes=(rbank,), sem=kb.ms["tensor"])
                            si = oi % 3
                            kb.cop("scalar", lambda e, b=bank, s=stgf[si], o_=o: e.activation(
                                s[:, :W], b[:, :W], AF.Identity, bias=k_bbase[:, o_:o_ + 1]),
                                reads=(rbank, rK), writes=(rstgf[si],))
                            dst = _ap(s_ffn.tensor, o * 128 * T + tj * W, [[T, 128], [1, W]])
                            kb.do("sync", lambda e, dd=dst, s=stgf[si]: e.dma_start(out=dd, in_=s[:, :W]),
                                  reads=(rstgf[si],), writes=(rD["s_ffn"],), sem=sst(f"stg9f{si}"), d=16)
                        elif o == 32:
                            kb.cop("scalar", lambda e, b=bank: e.activation(
                                hbuf[:, :W], b[:, :W], AF.Identity),
                                reads=(rbank,), writes=(rhbuf,))
                        else:
                            kb.cop("scalar", lambda e, b=bank: e.activation(
                                glb[:, :W], b[:, :W], AF.Identity),
                                reads=(rbank,), writes=(rglb,))
                            for w4 in range(W // 128):
                                kb.cop("tensor", lambda e, w_=w4: e.transpose(
                                    pA[2][:, 0:128], glb[:, w_ * 128:(w_ + 1) * 128], k_identf[:, :]),
                                    reads=(rglb, rK), writes=(rpA[2],))
                                kb.cop("vector", lambda e: e.tensor_copy(gt[:, :], pA[2][:, 0:128]),
                                    reads=(rpA[2],), writes=(rgt,))
                                kb.cop("vector", lambda e: e.reduce_max(
                                    gcol[:, 0:1], gt[:, 0:8], axis=mybir.AxisListType.X),
                                    reads=(rgt,), writes=(rgcol,))
                                kb.cop("vector", lambda e: e.tensor_scalar(
                                    e8[:, :], gt[:, 0:8], gcol[:, 0:1], None, op0=ALU.subtract),
                                    reads=(rgt, rgcol), writes=(re8,))
                                kb.cop("scalar", lambda e: e.activation(e8[:, :], e8[:, :], AF.Exp),
                                    reads=(re8,), writes=(re8,))
                                kb.cop("vector", lambda e: e.reduce_sum(
                                    gcol[:, 1:2], e8[:, :], axis=mybir.AxisListType.X),
                                    reads=(re8,), writes=(rgcol,))
                                kb.cop("vector", lambda e: e.reciprocal(gcol[:, 1:2], gcol[:, 1:2]),
                                    reads=(rgcol,), writes=(rgcol,))
                                kb.cop("scalar", lambda e: e.activation(
                                    gcol[:, 2:3], gt[:, 8:9], AF.Sigmoid, bias=k_bthr[:, 0:1]),
                                    reads=(rgt, rK), writes=(rgcol,))
                                kb.cop("vector", lambda e: e.tensor_scalar(
                                    gcol[:, 2:3], gcol[:, 2:3], 0.125, None, op0=ALU.mult),
                                    reads=(rgcol,), writes=(rgcol,))
                                rept = _ap(gcol, 2, [[4, 128], [0, 8]])
                                kb.cop("vector", lambda e, r=rept: e.scalar_tensor_tensor(
                                    e8[:, :], e8[:, :], gcol[:, 1:2], r,
                                    op0=ALU.mult, op1=ALU.subtract),
                                    reads=(re8, rgcol), writes=(re8,))
                                kb.cop("vector", lambda e: e.tensor_scalar(
                                    e8[:, :], e8[:, :], 0.0, None, op0=ALU.max),
                                    reads=(re8,), writes=(re8,))
                                kb.cop("vector", lambda e: e.reduce_sum(
                                    gcol[:, 0:1], e8[:, :], axis=mybir.AxisListType.X),
                                    reads=(re8,), writes=(rgcol,))
                                kb.cop("vector", lambda e: e.tensor_scalar(
                                    gcol[:, 3:4], gcol[:, 0:1], 0.0, None, op0=ALU.is_equal),
                                    reads=(rgcol,), writes=(rgcol,))
                                kb.cop("vector", lambda e: e.tensor_tensor(
                                    gcol[:, 0:1], gcol[:, 0:1], gcol[:, 3:4], op=ALU.add),
                                    reads=(rgcol,), writes=(rgcol,))
                                kb.cop("vector", lambda e: e.reciprocal(gcol[:, 0:1], gcol[:, 0:1]),
                                    reads=(rgcol,), writes=(rgcol,))
                                kb.cop("vector", lambda e: e.tensor_scalar(
                                    wn[:, 0:8], e8[:, :], gcol[:, 0:1], 2.0,
                                    op0=ALU.mult, op1=ALU.mult),
                                    reads=(re8, rgcol), writes=(rwn,))
                                kb.cop("tensor", lambda e: e.transpose(
                                    pA[2][:, 128:256], wn[:, :], k_identf[:, :]),
                                    reads=(rwn, rK), writes=(rpA[2],))
                                kb.cop("vector", lambda e: e.tensor_copy(wfmb[:, :], pA[2][0:8, 128:256]),
                                    reads=(rpA[2],), writes=(rwfmb,))
                                kb.cop("tensor", lambda e: e.matmul(
                                    pA[2][:, 256:384], k_mmat[:, :], wfmb[:, :],
                                    start=True, stop=True),
                                    reads=(rwfmb, rK), writes=(rpA[2],))
                                kb.cop("vector", lambda e, w_=w4: e.tensor_tensor(
                                    hwb[:, w_ * 128:(w_ + 1) * 128], hbuf[:, w_ * 128:(w_ + 1) * 128],
                                    pA[2][:, 256:384], op=ALU.mult),
                                    reads=(rhbuf, rpA[2]), writes=(rhwb,))
            kb.barrier()
            with ExitStack() as sk:
                lb = alloc_ln_bufs(sk, "_s9b", with_out=True)
                for tj in range(NTILE):
                    emit_ln(lb, s_ffn, rD["s_ffn"], T, tj * W, W, DC, "resid",
                            resid_dram=(scr["s_x2"], rD["s_x2"]), dst_dram=out_t, rdst=rD["out_t"])
            kb.barrier()

        s0()
        mixer_front()
        if upto not in ("inproj", "conv"):
            ssd_phase_a()
        if upto in ("all", "normed"):
            phase_b_gating()
        if upto == "all":
            out_proj_stage()
            ffn_stage()

        kb.barrier()

        block = st.enter_context(nc.Block())
        kb.playback(block)

    return nc


def host_prepare(inputs, T=2048):
    x = np.asarray(inputs["L0_attn_hidden_states"])
    B, S, _ = x.shape
    win = np.asarray(inputs["L0_attn_in_proj_w"])
    wout = np.asarray(inputs["L0_attn_out_proj_w"])
    convw_ = np.ascontiguousarray(np.asarray(inputs["L0_attn_conv_w"]).reshape(CC, 128, K)
                                  .transpose(1, 0, 2).reshape(128, CC * K).astype(np.float32))
    convb_ = np.ascontiguousarray(np.asarray(inputs["L0_attn_conv_b"]).reshape(CC, 128).T.astype(np.float32))
    dtb = np.asarray(inputs["L0_attn_dt_bias"]).reshape(NH, 1).astype(np.float32)
    Dd = np.asarray(inputs["L0_attn_D"]).reshape(NH)
    nw = np.asarray(inputs["L0_attn_norm_w"]).reshape(INTER)
    wb = np.asarray(inputs["L0_ffn_W_base"])
    bb = np.ascontiguousarray(np.asarray(inputs["L0_ffn_b_base"]).reshape(DC, 128).T.astype(np.float32))
    wg = np.asarray(inputs["L0_ffn_W_gate"])
    wt = np.asarray(inputs["L0_ffn_W_thr"]).reshape(1, D)
    bt = np.asarray(inputs["L0_ffn_b_thr"]).reshape(1)
    aw = np.asarray(inputs["L0_ffn_A_w"])
    bw = np.asarray(inputs["L0_ffn_B_w"])

    win_t = np.ascontiguousarray(win.T.astype(np.float32))
    wout_t = np.ascontiguousarray(wout.T.astype(np.float32))
    ffn1 = np.concatenate([wb, aw.reshape(128, D), wg, wt,
                           np.zeros((FFN1 - 4233, D), np.float32)], axis=0).astype(np.float32)
    wffn1_t = np.ascontiguousarray(ffn1.T)
    bstack_ = np.ascontiguousarray(np.transpose(bw, (0, 2, 1)).reshape(128, D).astype(np.float32))
    acolv = -(np.arange(1, NH + 1, dtype=np.float32)).reshape(NH, 1)
    dexpv = np.ascontiguousarray(np.repeat(Dd, HD).reshape(IC, 128).T.astype(np.float32))
    nwexpv = np.ascontiguousarray(nw.reshape(IC, 128).T.astype(np.float32))
    bthrv = np.full((128, 1), bt[0], np.float32)
    triv = (np.arange(L)[None, :] >= np.arange(L)[:, None]).astype(bf16)
    mmatv = np.zeros((8, 128), bf16)
    for e_ in range(8):
        mmatv[e_, e_ * 16:(e_ + 1) * 16] = bf16(1.0)

    shared = dict(win_t=win_t, wout_t=wout_t, wffn1_t=wffn1_t, bstack=bstack_,
                  convw=convw_, convb=convb_, dtbias=dtb, acol=acolv, dexp=dexpv,
                  nwexp=nwexpv, bbase=bb, bthr=bthrv, tri=triv, mmat=mmatv,
                  identf=np.eye(128, dtype=np.float32),
                  dtm=np.ascontiguousarray(np.tile(Dd.astype(np.float32), (128, 1))),
                  identb=np.eye(128).astype(bf16),
                  onesb=np.ones((128, 1), bf16), onesrow=np.ones((1, 128), bf16))
    in_maps = []
    halfs = S // T
    for core in range(B * halfs):
        b, hh = core // halfs, core % halfs
        xt = np.ascontiguousarray(x[b, hh * T:(hh + 1) * T, :].T.astype(np.float32))
        if hh == 0:
            xh = np.zeros((D, 3), np.float32)
        else:
            xh = np.ascontiguousarray(x[b, hh * T - 3:hh * T, :].T.astype(np.float32))
        iso = np.full((128, 1), 1.0 if hh % 2 == 1 else 0.0, np.float32)
        m = dict(shared)
        m.update(xin=xt, xhalo=xh, isodd=iso)
        in_maps.append(m)
    return in_maps


def kernel(**inputs):
    from concourse.bass_utils import run_bass_kernel_spmd
    T = 2048
    in_maps = host_prepare(inputs, T)
    nc = build_program(T=T, n_pairs=4)
    res = run_bass_kernel_spmd(nc, in_maps, core_ids=list(range(8)))
    x = np.asarray(inputs["L0_attn_hidden_states"])
    B, S, _ = x.shape
    out = np.empty((B, S, D), np.float32)
    halfs = S // T
    for core in range(B * halfs):
        b, hh = core // halfs, core % halfs
        out[b, hh * T:(hh + 1) * T, :] = res.results[core]["out_t"].T
    return out


# version 11 (final)
# speedup vs baseline: 1.0757x; 1.0757x over previous
import numpy as np
import ml_dtypes
import concourse.bass as bass
import concourse.mybir as mybir

F32 = mybir.dt.float32
BF16 = mybir.dt.bfloat16
ALU = mybir.AluOpType
AF = mybir.ActivationFunctionType
bf16 = ml_dtypes.bfloat16

D = 4096
DC = D // 128
PROJ = 18560
POC = PROJ // 128
GATE = 8192
GC = GATE // 128
CONV = 10240
CC = CONV // 128
INTER = 8192
IC = INTER // 128
NH, HD, SS, NG, K = 128, 64, 128, 8, 4
L = 128
EPS = 1e-5
DT_MIN, DT_MAX = 1e-3, 100.0
FFN1 = 4352
FOC = FFN1 // 128
ENGINES = ("sync", "vector", "scalar", "tensor", "gpsimd")


class Sem:
    __slots__ = ("h", "n", "name")

    def __init__(self, handle, name):
        self.h = handle
        self.n = 0
        self.name = name

    def inc(self, d):
        self.n += d
        return self.n


class R:
    __slots__ = ("wmarks", "rmarks", "name")

    def __init__(self, name):
        self.name = name
        self.wmarks = []
        self.rmarks = []


class Kb:
    def __init__(self, nc, stack):
        self.nc = nc
        self.st = stack
        self.q = {k: [] for k in ENGINES}
        self.waited = {k: {} for k in ENGINES}
        self._sems = {}
        self._res = {}
        self.pending = {k: [] for k in ENGINES}
        self.ms = {e: self.sem("ms_" + e) for e in ("vector", "scalar", "tensor", "gpsimd")}

    def sem(self, name):
        if name not in self._sems:
            h = self.st.enter_context(self.nc.semaphore(name))
            self._sems[name] = Sem(h, name)
        return self._sems[name]

    def res(self, name):
        if name not in self._res:
            self._res[name] = R(name)
        return self._res[name]

    def wait(self, eng, sem, val):
        if val <= 0:
            return
        if self.waited[eng].get(sem.name, -1) >= val:
            return
        self.waited[eng][sem.name] = val
        self.q[eng].append(lambda e, s=sem.h, v=val: e.wait_ge(s, v))

    def dep(self, eng, marks):
        best = {}
        for (sem, val) in marks:
            if best.get(sem.name, (None, -1))[1] < val:
                best[sem.name] = (sem, val)
        for sem, val in best.values():
            self.wait(eng, sem, val)

    def do(self, eng, fn, reads=(), writes=(), sem=None, d=1):
        for r in reads:
            self.dep(eng, r.wmarks)
        for w in writes:
            self.dep(eng, w.wmarks)
            self.dep(eng, w.rmarks)
        if sem is None:
            self.q[eng].append(fn)
            for r in reads:
                self.pending[eng].append(("r", r))
            for w in writes:
                self.pending[eng].append(("w", w))
            return None
        v = sem.inc(d)
        self.q[eng].append(lambda e, f=fn, s=sem.h, dd=d: f(e).then_inc(s, dd))
        m = (sem, v)
        for kind, r in self.pending[eng]:
            if kind == "r":
                r.rmarks.append(m)
            else:
                r.wmarks.append(m)
        self.pending[eng] = []
        for r in reads:
            r.rmarks.append(m)
        for w in writes:
            w.wmarks = [m]
            w.rmarks = []
        return m

    def cop(self, eng, fn, reads=(), writes=(), d=1):
        return self.do(eng, fn, reads, writes, sem=self.ms[eng], d=d)

    def barrier(self):
        for e in ("vector", "scalar", "tensor", "gpsimd"):
            if self.pending[e]:
                self.cop(e, lambda en: en.nop())
        for e in ENGINES:
            for s in self._sems.values():
                if s.n > 0:
                    self.wait(e, s, s.n)

    def playback(self, block):
        @block.sync
        def _(e):
            for fn in self.q["sync"]:
                fn(e)

        @block.vector
        def _(e):
            for fn in self.q["vector"]:
                fn(e)

        @block.scalar
        def _(e):
            for fn in self.q["scalar"]:
                fn(e)

        @block.tensor
        def _(e):
            for fn in self.q["tensor"]:
                fn(e)

        @block.gpsimd
        def _(e):
            for fn in self.q["gpsimd"]:
                fn(e)


def _ap(t, offset, dims):
    return bass.AP(t, offset, [list(d) for d in dims])


def build_program(T=2048, n_pairs=4, upto="all"):
    nc = bass.Bass("TRN2", target_bir_lowering=False)
    W = min(512, T)
    NT = T // L
    NTILE = T // W
    TH = T + 3

    ext = {}
    def ein(name, shape, dt=F32):
        ext[name] = nc.dram_tensor(name, shape, dt, kind="ExternalInput").ap()
        return ext[name]

    xin = ein("xin", [D, T])
    xhalo = ein("xhalo", [D, 3])
    win_t = ein("win_t", [D, PROJ])
    wout_t = ein("wout_t", [INTER, D])
    wffn1_t = ein("wffn1_t", [D, FFN1])
    bstack = ein("bstack", [128, D])
    convw = ein("convw", [128, CC * K])
    convb = ein("convb", [128, CC])
    dtbias = ein("dtbias", [NH, 1])
    acol = ein("acol", [NH, 1])
    dexp = ein("dexp", [128, IC])
    nwexp = ein("nwexp", [128, IC])
    bbase = ein("bbase", [128, DC])
    bthr = ein("bthr", [128, 1])
    tri = ein("tri", [L, L], BF16)
    mmat = ein("mmat", [8, 128], BF16)
    identf = ein("identf", [128, 128], F32)
    identb = ein("identb", [128, 128], BF16)
    isodd = ein("isodd", [128, 1])
    dtm = ein("dtm", [128, 128])
    onesb = ein("onesb", [128, 1], BF16)
    onesrow = ein("onesrow", [1, 128], BF16)

    out_t = nc.dram_tensor("out_t", [D, T], F32, kind="ExternalOutput").ap()

    scr = {}
    def sdram(name, shape, dt=F32):
        scr[name] = nc.dram_tensor(name, shape, dt).ap()
        return scr[name]

    s_gate = sdram("s_gate", [GATE, T], BF16)
    s_xbc = sdram("s_xbc", [CONV, TH], BF16)
    s_xfm = sdram("s_xfm", [INTER, T], BF16)
    s_bfm = sdram("s_bfm", [NG * SS, T], BF16)
    s_cfm = sdram("s_cfm", [NG * SS, T], BF16)
    s_cs = sdram("s_cs", [NT, NH * L], F32)
    s_gexp = sdram("s_gexp", [NT, NH * L], BF16)
    s_gexp2 = sdram("s_gexp2", [NT, NH * L], BF16)
    s_gexpt = sdram("s_gexpt", [T, NH], BF16)
    s_ypart = sdram("s_ypart", [NT, 128 * IC * L], F32)
    s_bcf = sdram("s_bcf", [NT, 128 * 16 * L], BF16)
    s_normed = sdram("s_normed", [INTER, T], BF16)
    s_att = sdram("s_att", [D, T], F32)
    s_x2 = sdram("s_x2", [D, T], F32)
    s_ffn = sdram("s_ffn", [D, T], F32)
    s_carry = sdram("s_carry", [128, INTER], F32)
    s_carryag = sdram("s_carryag", [256, INTER], F32)

    from contextlib import ExitStack
    with ExitStack() as st:
        kb = Kb(nc, st)
        rD = {name: kb.res("d_" + name) for name in scr}
        rD["xin"] = kb.res("d_xin")
        rD["xhalo"] = kb.res("d_xhalo")
        rD["out_t"] = kb.res("d_out_t")

        def sb(stk, name, shape, dt=F32):
            return stk.enter_context(nc.sbuf_tensor(name, shape, dt))

        pA = [st.enter_context(nc.psum_tensor(f"pA{i}", [128, 512], F32)) for i in range(4)]
        rpA = [kb.res(f"pA{i}") for i in range(4)]
        pB = [st.enter_context(nc.psum_tensor(f"pB{i}", [128, 1024], F32)) for i in range(2)]
        rpB = [kb.res(f"pB{i}") for i in range(2)]

        k_dtbias = sb(st, "k_dtbias", [128, 1]); k_acol = sb(st, "k_acol", [128, 1])
        k_dexp = sb(st, "k_dexp", [128, IC]); k_nwexp = sb(st, "k_nwexp", [128, IC])
        k_bbase = sb(st, "k_bbase", [128, DC]); k_bthr = sb(st, "k_bthr", [128, 1])
        k_convb = sb(st, "k_convb", [128, CC]); k_convw = sb(st, "k_convw", [128, CC * K])
        k_tri = sb(st, "k_tri", [L, L], BF16); k_mmat = sb(st, "k_mmat", [8, 128], BF16)
        k_identf = sb(st, "k_identf", [128, 128]); k_identb = sb(st, "k_identb", [128, 128], BF16)
        k_isodd = sb(st, "k_isodd", [128, 1]); k_onesb = sb(st, "k_onesb", [128, 1], BF16)
        k_dtm = sb(st, "k_dtm", [128, 128])
        k_eps = sb(st, "k_eps", [128, 1]); k_one = sb(st, "k_one", [128, 1])
        k_onesrow = sb(st, "k_onesrow", [1, 128], BF16)
        rK = kb.res("konst")

        trow = sb(st, "trow", [1, W]); rtrow = kb.res("trow")
        trow2 = sb(st, "trow2", [1, W]); rtrow2 = kb.res("trow2")
        trowb = sb(st, "trowb", [1, W], BF16); rtrowb = kb.res("trowb")
        trowb2 = sb(st, "trowb2", [1, W], BF16); rtrowb2 = kb.res("trowb2")
        tchunk = sb(st, "tchunk", [128, W]); rtchunk = kb.res("tchunk")
        HC = 16

        def alloc_ln_bufs(sk, tag, with_out=False):
            b = {}
            b["xtile"] = [sb(sk, f"xtile{i}" + tag, [128, HC * W]) for i in range(2)]
            b["rxtile"] = [kb.res(f"xtile{i}" + tag) for i in range(2)]
            b["xsq"] = sb(sk, "xsq" + tag, [128, HC * W], BF16); b["rxsq"] = kb.res("xsq" + tag)
            if with_out:
                b["xbfb"] = sb(sk, "xbfb" + tag, [128, HC * W], BF16); b["rxbfb"] = kb.res("xbfb" + tag)
            b["rsb"] = sb(sk, "rsb" + tag, [128, W]); b["rrsb"] = kb.res("rsb" + tag)
            b["msb"] = sb(sk, "msb" + tag, [128, W]); b["rmsb"] = kb.res("msb" + tag)
            b["cnt"] = [0]
            if with_out:
                b["xout"] = sb(sk, "xout" + tag, [128, HC * W]); b["rxout"] = kb.res("xout" + tag)
            return b

        def emit_ln(bufs, src_dram, rsrc, srcT, t0, w, nch, mode, out_sb=None, rout=None,
                    resid_dram=None, dst_dram=None, rdst=None, chunk_off=0):
            xbfb, rxbfb = bufs.get("xbfb"), bufs.get("rxbfb")
            xsq, rxsq = bufs["xsq"], bufs["rxsq"]
            rsbuf, rrsb = bufs["rsb"], bufs["rrsb"]
            msbuf, rmsb = bufs["msb"], bufs["rmsb"]
            xout, rxout = bufs.get("xout"), bufs.get("rxout")
            nhalf = (nch + HC - 1) // HC
            def half_load(hi):
                c0 = hi * HC
                ncc = min(HC, nch - c0)
                i = bufs["cnt"][0] % 2
                bufs["cnt"][0] += 1
                xt, rxt = bufs["xtile"][i], bufs["rxtile"][i]
                ap = _ap(src_dram.tensor, (chunk_off + c0) * 128 * srcT + t0,
                         [[srcT, 128], [128 * srcT, ncc], [1, w]])
                kb.do("sync", lambda e, s=ap, n=ncc, x=xt: e.dma_start(
                    out=x[:, :n * w].rearrange("p (c t) -> p c t", c=n), in_=s),
                    reads=(rsrc,), writes=(rxt,), sem=ld(xt.name), d=16)
                return ncc, xt, rxt
            for hi in range(nhalf):
                ncc, xt, rxt = half_load(hi)
                if mode == "bf16":
                    xb_t, rxb_t, xoff = out_sb, rout, hi * HC
                else:
                    xb_t, rxb_t, xoff = xbfb, rxbfb, 0
                kb.cop("scalar", lambda e, n=ncc, x=xt: e.activation(
                    xsq[:, :n * w], x[:, :n * w], AF.Square),
                    reads=(rxt,), writes=(rxsq,))
                kb.cop("vector", lambda e, n=ncc, x=xt, xb=xb_t, o=xoff: e.tensor_copy(
                    xb[:, o * w:(o + n) * w], x[:, :n * w]),
                    reads=(rxt,), writes=(rxb_t,))
                for kc in range(ncc):
                    kb.do("tensor", lambda e, k=kc, h0=hi, n=ncc, nh=nhalf, xb=xb_t, o=xoff: e.matmul(
                        pB[0][0:1, :w], k_onesb[:, 0:1], xb[:, (o + k) * w:(o + k + 1) * w],
                        start=(h0 == 0 and k == 0), stop=(h0 == nh - 1 and k == n - 1)),
                        reads=(rxb_t, rK), writes=(rpB[0],),
                        sem=(kb.ms["tensor"] if kc == ncc - 1 else None))
                for kc in range(ncc):
                    kb.do("tensor", lambda e, k=kc, h0=hi, n=ncc, nh=nhalf: e.matmul(
                        pB[1][0:1, :w], k_onesb[:, 0:1], xsq[:, k * w:(k + 1) * w],
                        start=(h0 == 0 and k == 0), stop=(h0 == nh - 1 and k == n - 1)),
                        reads=(rxsq, rK), writes=(rpB[1],),
                        sem=(kb.ms["tensor"] if kc == ncc - 1 else None))
            nd = float(nch * 128)
            kb.cop("vector", lambda e: e.tensor_scalar(
                trow[:, :w], pB[0][0:1, :w], 1.0 / nd, None, op0=ALU.mult),
                reads=(rpB[0],), writes=(rtrow,))
            kb.cop("vector", lambda e: e.tensor_tensor(
                trow2[:, :w], trow[:, :w], trow[:, :w], op=ALU.mult),
                reads=(rtrow,), writes=(rtrow2,))
            kb.cop("vector", lambda e: e.scalar_tensor_tensor(
                trow2[:, :w], pB[1][0:1, :w], 1.0 / nd, trow2[:, :w],
                op0=ALU.mult, op1=ALU.subtract),
                reads=(rpB[1], rtrow2), writes=(rtrow2,))
            kb.cop("scalar", lambda e: e.activation(
                trow2[:, :w], trow2[:, :w], AF.Sqrt, bias=k_eps[0:1, 0:1]),
                reads=(rtrow2, rK), writes=(rtrow2,))
            kb.cop("vector", lambda e: e.reciprocal(trow2[:, :w], trow2[:, :w]),
                reads=(rtrow2,), writes=(rtrow2,))
            kb.cop("vector", lambda e: e.tensor_copy(trowb[:, :w], trow2[:, :w]),
                reads=(rtrow2,), writes=(rtrowb,))
            kb.cop("vector", lambda e: e.tensor_tensor(
                trow[:, :w], trow[:, :w], trow2[:, :w], op=ALU.mult),
                reads=(rtrow, rtrow2), writes=(rtrow,))
            kb.cop("vector", lambda e: e.tensor_scalar(
                trowb2[:, :w], trow[:, :w], -1.0, None, op0=ALU.mult),
                reads=(rtrow,), writes=(rtrowb2,))
            kb.cop("tensor", lambda e: e.matmul(
                pA[2][:, :w], k_onesrow[:, :], trowb[:, :w], start=True, stop=True),
                reads=(rtrowb, rK), writes=(rpA[2],))
            kb.cop("tensor", lambda e: e.matmul(
                pA[3][:, :w], k_onesrow[:, :], trowb2[:, :w], start=True, stop=True),
                reads=(rtrowb2, rK), writes=(rpA[3],))
            kb.cop("vector", lambda e: e.tensor_copy(rsbuf[:, :w], pA[2][:, :w]),
                reads=(rpA[2],), writes=(rrsb,))
            kb.cop("vector", lambda e: e.tensor_copy(msbuf[:, :w], pA[3][:, :w]),
                reads=(rpA[3],), writes=(rmsb,))
            for hi in range(nhalf):
                ncc, xt, rxt = half_load(hi)
                for kc in range(ncc):
                    eng = "vector" if (kc % 3) else "gpsimd"
                    if mode == "bf16":
                        dst_sb, rdst_sb, off = out_sb, rout, (hi * HC + kc)
                    else:
                        dst_sb, rdst_sb, off = xout, rxout, kc
                    kb.cop(eng, lambda e, k=kc, x=xt, ds=dst_sb, o=off: e.tensor_tensor(
                        ds[:, o * w:(o + 1) * w],
                        x[:, k * w:(k + 1) * w], rsbuf[:, :w], op=ALU.mult),
                        reads=(rxt, rrsb), writes=(rdst_sb,))
                    kb.cop(eng, lambda e, ds=dst_sb, o=off: e.tensor_tensor(
                        ds[:, o * w:(o + 1) * w], ds[:, o * w:(o + 1) * w], msbuf[:, :w], op=ALU.add),
                        reads=(rdst_sb, rmsb), writes=(rdst_sb,))
                if mode == "resid":
                    c0 = hi * HC
                    rap = _ap(resid_dram[0].tensor, c0 * 128 * srcT + t0,
                              [[srcT, 128], [128 * srcT, ncc], [1, w]])
                    kb.do("gpsimd", lambda e, s=rap, n=ncc: e.dma_start(
                        out=xout[:, :n * w].rearrange("p (c t) -> p c t", c=n), in_=s,
                        accum_op=ALU.add),
                        reads=(resid_dram[1], rxout), writes=(rxout,), sem=ld("xout"), d=16)
                    dap = _ap(dst_dram.tensor, c0 * 128 * srcT + t0,
                              [[srcT, 128], [128 * srcT, ncc], [1, w]])
                    kb.do("sync", lambda e, dd=dap, n=ncc: e.dma_start(
                        out=dd, in_=xout[:, :n * w].rearrange("p (c t) -> p c t", c=n)),
                        reads=(rxout,), writes=(rdst,), sem=sst("xout"), d=16)

        sload = {}
        def ld(name):
            if name not in sload:
                sload[name] = kb.sem("ld_" + name)
            return sload[name]
        sstore = {}
        def sst(name):
            if name not in sstore:
                sstore[name] = kb.sem("st_" + name)
            return sstore[name]

        def s0():
            def dma(dst_ap, src_ap):
                kb.do("sync", lambda e, dd=dst_ap, ss=src_ap: e.dma_start(out=dd, in_=ss),
                      reads=(), writes=(rK,), sem=ld("konst"), d=16)
            dma(k_dtbias[:, :], dtbias[:, :]); dma(k_acol[:, :], acol[:, :])
            dma(k_bthr[:, :], bthr[:, :]); dma(k_tri[:, :], tri[:, :])
            dma(k_mmat[:, :], mmat[:, :]); dma(k_identf[:, :], identf[:, :])
            dma(k_identb[:, :], identb[:, :]); dma(k_isodd[:, :], isodd[:, :])
            dma(k_onesb[:, :], onesb[:, :])
            dma(k_dtm[:, :], dtm[:, :])
            dma(k_onesrow[:, :], onesrow[:, :])
            dma(k_dexp[:, :], dexp[:, :]); dma(k_nwexp[:, :], nwexp[:, :])
            dma(k_bbase[:, :], bbase[:, :]); dma(k_convb[:, :], convb[:, :])
            dma(k_convw[:, :], convw[:, :])
            kb.cop("vector", lambda e: e.memset(k_eps[:, :], EPS), writes=(rK,))
            kb.cop("vector", lambda e: e.memset(k_one[:, :], 1.0), writes=(rK,))

        def mixer_front():
            with ExitStack() as sk:
                lb = alloc_ln_bufs(sk, "_mf")
                xtile, rxtile = lb["xtile"][0], lb["rxtile"][0]
                xsq, rxsq = lb["xsq"], lb["rxsq"]
                lnx = [sb(sk, "lnxA", [128, DC * W], BF16), sb(sk, "lnxB", [128, DC * W], BF16)]
                rlnx = [kb.res("lnxA"), kb.res("lnxB")]
                lnxh = sb(sk, "lnxh", [128, DC * 3], BF16); rlnxh = kb.res("lnxh")
                wslab = [sb(sk, f"wslab{i}", [128, DC * 128], BF16) for i in range(2)]
                rwslab = [kb.res(f"wslab{i}") for i in range(2)]
                stg = [sb(sk, f"stg{i}", [128, W], BF16) for i in range(4)]
                rstg = [kb.res(f"stg{i}") for i in range(4)]
                hstg = sb(sk, "hstg", [128, 4], BF16); rhstg = kb.res("hstg")
                dtraw_l = sb(sk, "dtraw", [128, T]); rdtraw = kb.res("dtraw")

                def ln_halo():
                    ap = _ap(xhalo.tensor, 0, [[3, 128], [128 * 3, DC], [1, 3]])
                    kb.do("sync", lambda e, s=ap: e.dma_start(
                        out=xtile[:, :DC * 3].rearrange("p (c t) -> p c t", c=DC), in_=s),
                        reads=(rD["xhalo"],), writes=(rxtile,), sem=ld(xtile.name), d=16)
                    kb.cop("scalar", lambda e: e.activation(
                        xsq[:, :DC * 3], xtile[:, :DC * 3], AF.Square),
                        reads=(rxtile,), writes=(rxsq,))
                    kb.cop("vector", lambda e: e.tensor_copy(
                        lnxh[:, :DC * 3], xtile[:, :DC * 3]),
                        reads=(rxtile,), writes=(rlnxh,))
                    for kc in range(DC):
                        kb.do("tensor", lambda e, k=kc: e.matmul(
                            pB[0][0:1, :3], k_onesb[:, 0:1], lnxh[:, k * 3:(k + 1) * 3],
                            start=(k == 0), stop=(k == DC - 1)),
                            reads=(rlnxh, rK), writes=(rpB[0],),
                            sem=(kb.ms["tensor"] if kc == DC - 1 else None))
                    for kc in range(DC):
                        kb.do("tensor", lambda e, k=kc: e.matmul(
                            pB[1][0:1, :3], k_onesb[:, 0:1], xsq[:, k * 3:(k + 1) * 3],
                            start=(k == 0), stop=(k == DC - 1)),
                            reads=(rxsq, rK), writes=(rpB[1],),
                            sem=(kb.ms["tensor"] if kc == DC - 1 else None))
                    nd = float(D)
                    kb.cop("vector", lambda e: e.tensor_scalar(
                        trow[:, :3], pB[0][0:1, :3], 1.0 / nd, None, op0=ALU.mult),
                        reads=(rpB[0],), writes=(rtrow,))
                    kb.cop("vector", lambda e: e.tensor_tensor(
                        trow2[:, :3], trow[:, :3], trow[:, :3], op=ALU.mult),
                        reads=(rtrow,), writes=(rtrow2,))
                    kb.cop("vector", lambda e: e.scalar_tensor_tensor(
                        trow2[:, :3], pB[1][0:1, :3], 1.0 / nd, trow2[:, :3],
                        op0=ALU.mult, op1=ALU.subtract),
                        reads=(rpB[1], rtrow2), writes=(rtrow2,))
                    kb.cop("scalar", lambda e: e.activation(
                        trow2[:, :3], trow2[:, :3], AF.Sqrt, bias=k_eps[0:1, 0:1]),
                        reads=(rtrow2, rK), writes=(rtrow2,))
                    kb.cop("vector", lambda e: e.reciprocal(trow2[:, :3], trow2[:, :3]),
                        reads=(rtrow2,), writes=(rtrow2,))
                    kb.cop("vector", lambda e: e.tensor_copy(trowb[:, :3], trow2[:, :3]),
                        reads=(rtrow2,), writes=(rtrowb,))
                    kb.cop("vector", lambda e: e.tensor_tensor(
                        trow[:, :3], trow[:, :3], trow2[:, :3], op=ALU.mult),
                        reads=(rtrow, rtrow2), writes=(rtrow,))
                    kb.cop("vector", lambda e: e.tensor_scalar(
                        trowb2[:, :3], trow[:, :3], -1.0, None, op0=ALU.mult),
                        reads=(rtrow,), writes=(rtrowb2,))
                    kb.cop("tensor", lambda e: e.matmul(
                        pB[0][:, :3], k_onesrow[:, :], trowb[:, :3], start=True, stop=True),
                        reads=(rtrowb, rK), writes=(rpB[0],))
                    kb.cop("tensor", lambda e: e.matmul(
                        pB[1][:, :3], k_onesrow[:, :], trowb2[:, :3], start=True, stop=True),
                        reads=(rtrowb2, rK), writes=(rpB[1],))
                    for kc in range(DC):
                        kb.cop("vector", lambda e, k=kc: e.tensor_tensor(
                            tchunk[:, :3], xtile[:, k * 3:(k + 1) * 3], pB[0][:, :3], op=ALU.mult),
                            reads=(rxtile, rpB[0]), writes=(rtchunk,))
                        kb.cop("vector", lambda e, k=kc: e.tensor_tensor(
                            lnxh[:, k * 3:(k + 1) * 3], tchunk[:, :3], pB[1][:, :3], op=ALU.add),
                            reads=(rtchunk, rpB[1]), writes=(rlnxh,))

                def inproj_pass(tiles, do_halo):
                    nt = len(tiles)
                    for o in range(POC):
                        sl, rsl = wslab[o % 2], rwslab[o % 2]
                        ap = _ap(win_t.tensor, o * 128, [[PROJ, 128], [128 * PROJ, DC], [1, 128]])
                        kb.do("gpsimd", lambda e, s=ap, dd=sl: e.dma_start(
                            out=dd[:].rearrange("p (c o) -> p c o", c=DC), in_=s),
                            reads=(), writes=(rsl,), sem=ld(f"wslab{o % 2}"), d=16)
                        is_gate = o < GC
                        is_dt = o == POC - 1
                        xbc_o = o - GC
                        for ti in range(nt):
                            lx, rlx = lnx[ti], rlnx[ti]
                            bi = (o * nt + ti) % 4
                            bank, rbank = pA[bi], rpA[bi]
                            for kc in range(DC):
                                kb.do("tensor", lambda e, k=kc, b=bank, s=sl, x=lx: e.matmul(
                                    b[:, :W], s[:, k * 128:(k + 1) * 128], x[:, k * W:(k + 1) * W],
                                    start=(k == 0), stop=(k == DC - 1)),
                                    reads=(rsl, rlx), writes=(rbank,),
                                    sem=(kb.ms["tensor"] if kc == DC - 1 else None))
                            if is_dt:
                                kb.cop("scalar", lambda e, b=bank, t0=tiles[ti] * W: e.activation(
                                    dtraw_l[:, t0:t0 + W], b[:, :W], AF.Identity),
                                    reads=(rbank,), writes=(rdtraw,))
                            else:
                                stb, rstb = stg[bi], rstg[bi]
                                fn = AF.Silu if is_gate else AF.Identity
                                kb.cop("scalar", lambda e, b=bank, s=stb, f=fn: e.activation(
                                    s[:, :W], b[:, :W], f),
                                    reads=(rbank,), writes=(rstb,))
                                if is_gate:
                                    dst = _ap(s_gate.tensor, o * 128 * T + tiles[ti] * W, [[T, 128], [1, W]])
                                    rdst = rD["s_gate"]
                                else:
                                    dst = _ap(s_xbc.tensor, xbc_o * 128 * TH + 3 + tiles[ti] * W, [[TH, 128], [1, W]])
                                    rdst = rD["s_xbc"]
                                kb.do("sync", lambda e, dd=dst, s=stb: e.dma_start(out=dd, in_=s[:, :W]),
                                      reads=(rstb,), writes=(rdst,), sem=sst(f"stg{bi}"), d=16)
                        if do_halo and (not is_gate) and (not is_dt):
                            for kc in range(DC):
                                kb.do("tensor", lambda e, k=kc, s=sl: e.matmul(
                                    pB[1][:, 512:515], s[:, k * 128:(k + 1) * 128], lnxh[:, k * 3:(k + 1) * 3],
                                    start=(k == 0), stop=(k == DC - 1)),
                                    reads=(rsl, rlnxh), writes=(rpB[1],),
                                    sem=(kb.ms["tensor"] if kc == DC - 1 else None))
                            kb.cop("scalar", lambda e: e.activation(
                                hstg[:, 0:3], pB[1][:, 512:515], AF.Identity),
                                reads=(rpB[1],), writes=(rhstg,))
                            dst = _ap(s_xbc.tensor, xbc_o * 128 * TH, [[TH, 128], [1, 3]])
                            kb.do("sync", lambda e, dd=dst: e.dma_start(out=dd, in_=hstg[:, 0:3]),
                                  reads=(rhstg,), writes=(rD["s_xbc"],), sem=sst("hstg"), d=16)

                def dt_softplus():
                    dts1 = sb(sk, "dts1", [128, T]); rdts1 = kb.res("dts1")
                    kb.cop("vector", lambda e: e.tensor_scalar(
                        dtraw_l[:, :T], dtraw_l[:, :T], k_dtbias[:, 0:1], None, op0=ALU.add),
                        reads=(rdtraw, rK), writes=(rdtraw,))
                    kb.cop("scalar", lambda e: e.activation(dts1[:, :T], dtraw_l[:, :T], AF.Abs),
                        reads=(rdtraw,), writes=(rdts1,))
                    kb.cop("scalar", lambda e: e.activation(dts1[:, :T], dts1[:, :T], AF.Exp, scale=-1.0),
                        reads=(rdts1,), writes=(rdts1,))
                    kb.cop("scalar", lambda e: e.activation(dts1[:, :T], dts1[:, :T], AF.Ln, bias=k_one[:, 0:1]),
                        reads=(rdts1, rK), writes=(rdts1,))
                    kb.cop("vector", lambda e: e.tensor_scalar(
                        dtraw_l[:, :T], dtraw_l[:, :T], 0.0, None, op0=ALU.max),
                        reads=(rdtraw,), writes=(rdtraw,))
                    kb.cop("vector", lambda e: e.tensor_tensor(
                        dtraw_l[:, :T], dtraw_l[:, :T], dts1[:, :T], op=ALU.add),
                        reads=(rdtraw, rdts1), writes=(rdtraw,))
                    kb.cop("vector", lambda e: e.tensor_scalar(
                        dtraw_l[:, :T], dtraw_l[:, :T], DT_MIN, DT_MAX, op0=ALU.max, op1=ALU.min),
                        reads=(rdtraw,), writes=(rdtraw,))
                    kb.do("sync", lambda e: e.dma_start(out=scr["s_dt"], in_=dtraw_l[:, :T]),
                          reads=(rdtraw,), writes=(rD["s_dt"],), sem=sst("dtraw"), d=16)

                def conv():
                    dgtile_ = [sb(sk, f"dgtile{i}", [128, K * 128], BF16) for i in range(2)]
                    rdgtile_ = [kb.res(f"dgtile{i}") for i in range(2)]
                    xbcw_ = [sb(sk, f"xbcw{i}", [128, W + 3], BF16) for i in range(4)]
                    rxbcw_ = [kb.res(f"xbcw{i}") for i in range(4)]
                    for ch in range(CC):
                        dgtile, rdgtile = dgtile_[ch % 2], rdgtile_[ch % 2]
                        for j in range(K):
                            kb.cop("vector", lambda e, jj=j, ch_=ch, dg=dgtile: e.tensor_scalar(
                                dg[:, jj * 128:(jj + 1) * 128], k_identb[:, :],
                                k_convw[:, ch_ * K + jj:ch_ * K + jj + 1], None, op0=ALU.mult),
                                reads=(rK,), writes=(rdgtile,))
                        for tj in range(NTILE):
                            src = _ap(s_xbc.tensor, ch * 128 * TH + tj * W, [[TH, 128], [1, W + 3]])
                            wbi = (ch * NTILE + tj) % 4
                            xbcw, rxbcw = xbcw_[wbi], rxbcw_[wbi]
                            kb.do("sync", lambda e, s=src, xw=xbcw: e.dma_start(out=xw[:, :W + 3], in_=s),
                                  reads=(rD["s_xbc"],), writes=(rxbcw,), sem=ld(f"xbcw{wbi}"), d=16)
                            bi = (ch * NTILE + tj) % 3
                            bank, rbank = pA[bi], rpA[bi]
                            for j in range(K):
                                kb.do("tensor", lambda e, jj=j, b=bank, dg=dgtile, xw=xbcw: e.matmul(
                                    b[:, :W], dg[:, jj * 128:(jj + 1) * 128], xw[:, jj:jj + W],
                                    start=(jj == 0), stop=(jj == K - 1)),
                                    reads=(rdgtile, rxbcw), writes=(rbank,),
                                    sem=(kb.ms["tensor"] if j == K - 1 else None))
                            stb, rstb = stg[(ch * NTILE + tj) % 4], rstg[(ch * NTILE + tj) % 4]
                            kb.cop("scalar", lambda e, b=bank, s=stb, ch_=ch: e.activation(
                                s[:, :W], b[:, :W], AF.Silu, bias=k_convb[:, ch_:ch_ + 1]),
                                reads=(rbank, rK), writes=(rstb,))
                            if ch < IC:
                                dst = _ap(s_xfm.tensor, ch * 128 * T + tj * W, [[T, 128], [1, W]])
                                kb.do("sync", lambda e, dd=dst, s=stb: e.dma_start(out=dd, in_=s[:, :W]),
                                      reads=(rstb,), writes=(rD["s_xfm"],), sem=sst(stb.name), d=16)
                            else:
                                if ch < IC + 8:
                                    dst = _ap(s_bfm.tensor, (ch - IC) * 128 * T + tj * W, [[T, 128], [1, W]])
                                    kb.do("sync", lambda e, dd=dst, s=stb: e.dma_start(out=dd, in_=s[:, :W]),
                                          reads=(rstb,), writes=(rD["s_bfm"],), sem=sst(stb.name), d=16)
                                c0 = tj * W // L
                                ncl = W // L
                                dst2 = _ap(s_bcf.tensor, c0 * 128 * 16 * L + (ch - IC) * L,
                                           [[16 * L, 128], [128 * 16 * L, ncl], [1, L]])
                                kb.do("sync", lambda e, dd=dst2, s=stb, n=ncl: e.dma_start(
                                    out=dd, in_=s[:, :W].rearrange("p (c l) -> p c l", c=n)),
                                    reads=(rstb,), writes=(rD["s_bcf"],), sem=sst("b" + stb.name), d=16)

                ln_halo()
                half = max(1, NTILE // 2)
                passes = [p for p in (list(range(0, half)), list(range(half, NTILE))) if p]
                for pi, ptiles in enumerate(passes):
                    for i, tj in enumerate(ptiles):
                        emit_ln(lb, xin, rD["xin"], T, tj * W, W, DC, "bf16", out_sb=lnx[i], rout=rlnx[i])
                    inproj_pass(ptiles, do_halo=(pi == 0))
                dt_softplus()
                conv()
            kb.barrier()

        def ssd_phase_a():
            with ExitStack() as sk:
                NB = 2
                zeros128 = sb(sk, "zeros128", [128, L]); rzeros = kb.res("zeros128")
                cumlog = sb(sk, "cumlog", [128, 1]); rcumlog = kb.res("cumlog")
                carry = sb(sk, "carry", [128, INTER]); rcarry = kb.res("carry")
                prevb = sb(sk, "prevb", [128, INTER], BF16); rprevb = kb.res("prevb")
                ctmp = sb(sk, "ctmp", [128, 1024]); rctmp = kb.res("ctmp")
                x_tm = sb(sk, "x_tm", [128, INTER], BF16); rx_tm = kb.res("x_tm")
                xddb = sb(sk, "xddb", [128, INTER], BF16); rxddb = kb.res("xddb")
                dtTf = sb(sk, "dtTf", [128, L]); rdtTf = kb.res("dtTf")
                csT2 = [sb(sk, f"csT2_{i}", [128, L]) for i in range(2)]
                rcsT2 = [kb.res(f"csT2_{i}") for i in range(2)]
                b_tm = sb(sk, "b_tm", [128, NG * SS], BF16); rb_tm = kb.res("b_tm")
                ystg = [sb(sk, f"ystg{i}", [128, 512]) for i in range(2)]
                rystg = [kb.res(f"ystg{i}") for i in range(2)]
                dtc = [sb(sk, f"dtc{i}", [128, L]) for i in range(NB)]
                rdtc = [kb.res(f"dtc{i}") for i in range(NB)]
                dAc = [sb(sk, f"dAc{i}", [128, L]) for i in range(NB)]
                rdAc = [kb.res(f"dAc{i}") for i in range(NB)]
                csc = [sb(sk, f"csc{i}", [128, L]) for i in range(NB)]
                rcsc = [kb.res(f"csc{i}") for i in range(NB)]
                csTf = [sb(sk, f"csTf{i}", [128, L]) for i in range(NB)]
                rcsTf = [kb.res(f"csTf{i}") for i in range(NB)]
                gexpb = [sb(sk, f"gexpb{i}", [128, L], BF16) for i in range(NB)]
                rgexpb = [kb.res(f"gexpb{i}") for i in range(NB)]
                gexp2b = [sb(sk, f"gexp2b{i}", [128, L], BF16) for i in range(NB)]
                rgexp2b = [kb.res(f"gexp2b{i}") for i in range(NB)]
                gexptb = [sb(sk, f"gexptb{i}", [128, L], BF16) for i in range(NB)]
                rgexptb = [kb.res(f"gexptb{i}") for i in range(NB)]
                cdexp = [sb(sk, f"cdexp{i}", [128, L], BF16) for i in range(NB)]
                rcdexp = [kb.res(f"cdexp{i}") for i in range(NB)]
                dsf = [sb(sk, f"dsf{i}", [128, L]) for i in range(NB)]
                rdsf = [kb.res(f"dsf{i}") for i in range(NB)]
                ddTb = [sb(sk, f"ddTb{i}", [128, L], BF16) for i in range(NB)]
                rddTb = [kb.res(f"ddTb{i}") for i in range(NB)]
                bcw = [sb(sk, f"bcw{i}", [128, 16 * L], BF16) for i in range(NB)]
                rbcw = [kb.res(f"bcw{i}") for i in range(NB)]
                xD = sb(sk, "xD", [128, INTER], BF16); rxD = kb.res("xD")
                csb = [sb(sk, f"csb{i}", [128, 16 * L]) for i in range(3)]
                rcsb = [kb.res(f"csb{i}") for i in range(3)]
                Eb = [sb(sk, f"Eb{i}", [128, 16 * L], BF16) for i in range(3)]
                rEb = [kb.res(f"Eb{i}") for i in range(3)]
                scT = [sb(sk, f"scT{i}", [128, 16 * L], BF16) for i in range(2)]
                rscT = [kb.res(f"scT{i}") for i in range(2)]
                e2b = [sb(sk, f"e2b{i}", [128, 16 * L], BF16) for i in range(3)]
                re2b = [kb.res(f"e2b{i}") for i in range(3)]
                cgb = [sb(sk, f"cgb{i}", [128, 16 * L], BF16) for i in range(2)]
                rcgb = [kb.res(f"cgb{i}") for i in range(2)]
                cbm = [sb(sk, f"cbm{i}", [128, L], BF16) for i in range(2)]
                rcbm = [kb.res(f"cbm{i}") for i in range(2)]

                kb.cop("vector", lambda e: e.memset(zeros128[:, :], 0.0), writes=(rzeros,))
                kb.cop("vector", lambda e: e.memset(cumlog[:, :], 0.0), writes=(rcumlog,))
                kb.cop("vector", lambda e: e.memset(carry[:, :], 0.0), writes=(rcarry,))

                for ci in range(NT):
                    t0 = ci * L
                    p = ci % NB
                    kb.do("sync", lambda e, t=t0, b=dtc[p]: e.dma_start(
                        out=b[:, :], in_=_ap(scr["s_dt"].tensor, t, [[T, 128], [1, L]])),
                        reads=(rD["s_dt"],), writes=(rdtc[p],), sem=ld(f"dtc{p}"), d=16)
                    kb.do("sync", lambda e, t=t0, b=x_tm: e.dma_start(
                        out=b[:, :], in_=_ap(s_xfm.tensor, t, [[T, INTER], [1, L]]), transpose=True),
                        reads=(rD["s_xfm"],), writes=(rx_tm,), sem=ld("x_tm"), d=16)
                    kb.do("sync", lambda e, t=t0, b=b_tm: e.dma_start(
                        out=b[:, :], in_=_ap(s_bfm.tensor, t, [[T, NG * SS], [1, L]]), transpose=True),
                        reads=(rD["s_bfm"],), writes=(rb_tm,), sem=ld("b_tm"), d=16)
                    kb.do("sync", lambda e, c=ci, b=bcw[p]: e.dma_start(
                        out=b[:, :], in_=_ap(s_bcf.tensor, c * 128 * 16 * L, [[16 * L, 128], [1, 16 * L]])),
                        reads=(rD["s_bcf"],), writes=(rbcw[p],), sem=ld(f"bcw{p}"), d=16)
                    kb.cop("vector", lambda e, b=dtc[p], o=dAc[p]: e.tensor_scalar(
                        o[:, :], b[:, :], k_acol[:, 0:1], None, op0=ALU.mult),
                        reads=(rdtc[p], rK), writes=(rdAc[p],))
                    kb.cop("vector", lambda e, i=p: e.tensor_tensor_scan(
                        csc[i][:, :], dAc[i][:, :], zeros128[:, :], 0.0, op0=ALU.add, op1=ALU.add),
                        reads=(rdAc[p], rzeros), writes=(rcsc[p],))
                    kb.do("sync", lambda e, c=ci, i=p: e.dma_start(
                        out=_ap(s_cs.tensor, c * NH * L, [[L, 128], [1, L]]), in_=csc[i][:, :]),
                        reads=(rcsc[p],), writes=(rD["s_cs"],), sem=sst(f"csc{p}"), d=16)
                    kb.cop("scalar", lambda e, i=p: e.activation(gexpb[i][:, :], csc[i][:, :], AF.Exp),
                        reads=(rcsc[p],), writes=(rgexpb[p],))
                    kb.do("sync", lambda e, c=ci, i=p: e.dma_start(
                        out=_ap(s_gexp.tensor, c * NH * L, [[L, 128], [1, L]]), in_=gexpb[i][:, :]),
                        reads=(rgexpb[p],), writes=(rD["s_gexp"],), sem=sst(f"gexpb{p}"), d=16)
                    kb.cop("scalar", lambda e, i=p: e.activation(
                        gexp2b[i][:, :], csc[i][:, :], AF.Exp, bias=cumlog[:, 0:1]),
                        reads=(rcsc[p], rcumlog), writes=(rgexp2b[p],))
                    kb.do("sync", lambda e, c=ci, i=p: e.dma_start(
                        out=_ap(s_gexp2.tensor, c * NH * L, [[L, 128], [1, L]]), in_=gexp2b[i][:, :]),
                        reads=(rgexp2b[p],), writes=(rD["s_gexp2"],), sem=sst(f"gexp2b{p}"), d=16)
                    kb.cop("vector", lambda e, i=p: e.tensor_tensor(
                        cumlog[:, 0:1], cumlog[:, 0:1], csc[i][:, L - 1:L], op=ALU.add),
                        reads=(rcumlog, rcsc[p]), writes=(rcumlog,))
                    kb.cop("tensor", lambda e, i=p: e.transpose(pA[0][:, 0:L], csc[i][:, :], k_identf[:, :]),
                        reads=(rcsc[p], rK), writes=(rpA[0],))
                    kb.cop("vector", lambda e, i=p: e.tensor_copy(csTf[i][:, :], pA[0][:, 0:L]),
                        reads=(rpA[0],), writes=(rcsTf[p],))
                    kb.cop("scalar", lambda e, i=p: e.activation(gexptb[i][:, :], csTf[i][:, :], AF.Exp),
                        reads=(rcsTf[p],), writes=(rgexptb[p],))
                    kb.do("sync", lambda e, t=t0, i=p: e.dma_start(
                        out=_ap(s_gexpt.tensor, t * NH, [[NH, 128], [1, NH]]), in_=gexptb[i][:, :]),
                        reads=(rgexptb[p],), writes=(rD["s_gexpt"],), sem=sst(f"gexptb{p}"), d=16)
                    kb.do("sync", lambda e, t=t0, i=p: e.dma_start(
                        out=cdexp[i][:, :], in_=_ap(s_gexpt.tensor, (t + L - 1) * NH, [[0, 128], [1, NH]])),
                        reads=(rD["s_gexpt"],), writes=(rcdexp[p],), sem=ld(f"cdexp{p}"), d=16)
                    kb.cop("scalar", lambda e, i=p: e.activation(
                        dsf[i][:, :], csc[i][:, :], AF.Exp, scale=-1.0, bias=csc[i][:, L - 1:L]),
                        reads=(rcsc[p],), writes=(rdsf[p],))
                    kb.cop("vector", lambda e, i=p: e.tensor_tensor(
                        dsf[i][:, :], dtc[i][:, :], dsf[i][:, :], op=ALU.mult),
                        reads=(rdtc[p], rdsf[p]), writes=(rdsf[p],))
                    kb.cop("tensor", lambda e, i=p: e.transpose(pA[0][:, 128:256], dsf[i][:, :], k_identf[:, :]),
                        reads=(rdsf[p], rK), writes=(rpA[0],))
                    kb.cop("vector", lambda e, i=p: e.tensor_copy(ddTb[i][:, :], pA[0][:, 128:256]),
                        reads=(rpA[0],), writes=(rddTb[p],))
                    kb.cop("tensor", lambda e, i=p: e.transpose(pA[0][:, 256:384], dtc[i][:, :], k_identf[:, :]),
                        reads=(rdtc[p], rK), writes=(rpA[0],))
                    kb.cop("vector", lambda e: e.tensor_copy(dtTf[:, :], pA[0][:, 256:384]),
                        reads=(rpA[0],), writes=(rdtTf,))
                    kb.cop("scalar", lambda e, i=p: e.activation(csT2[i][:, :], dtTf[:, :], AF.Ln),
                        reads=(rdtTf,), writes=(rcsT2[p],))
                    kb.cop("vector", lambda e, i=p: e.tensor_tensor(
                        csT2[i][:, :], csTf[i][:, :], csT2[i][:, :], op=ALU.subtract),
                        reads=(rcsTf[p], rcsT2[p]), writes=(rcsT2[p],))
                    rep = _ap(ddTb[p], 0, [[L, 128], [1, 128], [0, 64]])
                    kb.cop("vector", lambda e, r=rep: e.tensor_tensor(
                        xddb[:].rearrange("p (h q) -> p h q", h=128),
                        x_tm[:].rearrange("p (h q) -> p h q", h=128), r, op=ALU.mult),
                        reads=(rx_tm, rddTb[p]), writes=(rxddb,))
                    for g in range(NG):
                        stp, rstp = pB[g % 2], rpB[g % 2]
                        kb.do("tensor", lambda e, gg=g, b=stp: e.matmul(
                            b[:, 0:512], b_tm[:, gg * 128:(gg + 1) * 128],
                            xddb[:, gg * 1024:gg * 1024 + 512], start=True, stop=True),
                            reads=(rb_tm, rxddb), writes=(rstp,), sem=None)
                        kb.do("tensor", lambda e, gg=g, b=stp: e.matmul(
                            b[:, 512:1024], b_tm[:, gg * 128:(gg + 1) * 128],
                            xddb[:, gg * 1024 + 512:(gg + 1) * 1024], start=True, stop=True),
                            reads=(rb_tm, rxddb), writes=(rstp,), sem=kb.ms["tensor"])
                        kb.cop("vector", lambda e, gg=g: e.tensor_copy(
                            prevb[:, gg * 1024:(gg + 1) * 1024], carry[:, gg * 1024:(gg + 1) * 1024]),
                            reads=(rcarry,), writes=(rprevb,))
                        repcd = _ap(cdexp[p], g * 16, [[L, 128], [1, 16], [0, 64]])
                        kb.cop("vector", lambda e, gg=g, r=repcd: e.tensor_tensor(
                            ctmp[:].rearrange("p (h q) -> p h q", h=16),
                            carry[:, gg * 1024:(gg + 1) * 1024].rearrange("p (h q) -> p h q", h=16),
                            r, op=ALU.mult),
                            reads=(rcarry, rcdexp[p]), writes=(rctmp,))
                        kb.cop("vector", lambda e, gg=g, b=stp: e.tensor_tensor(
                            carry[:, gg * 1024:(gg + 1) * 1024], ctmp[:, :], b[:, :], op=ALU.add),
                            reads=(rctmp, rstp), writes=(rcarry,))
                    repD = _ap(k_dtm, 0, [[128, 128], [1, 128], [0, 64]])
                    kb.cop("vector", lambda e, r=repD: e.tensor_tensor(
                        xD[:].rearrange("p (h q) -> p h q", h=128),
                        x_tm[:].rearrange("p (h q) -> p h q", h=128), r, op=ALU.mult),
                        reads=(rx_tm, rK), writes=(rxD,))
                    def seg_chain(g):
                        q = g % 3
                        kb.do("sync", lambda e, gg=g, c=ci, b=csb[q]: e.dma_start(
                            out=b[:, :],
                            in_=_ap(s_cs.tensor, c * NH * L + gg * 16 * L, [[0, 128], [1, 16 * L]])),
                            reads=(rD["s_cs"],), writes=(rcsb[q],), sem=ld(f"csb{q}"), d=16)
                        repct = _ap(csT2[p], g * 16, [[L, 128], [1, 16], [0, 128]])
                        kb.cop("gpsimd", lambda e, b=csb[q], r=repct: e.tensor_tensor(
                            b[:].rearrange("p (h q2) -> p h q2", h=16),
                            b[:].rearrange("p (h q2) -> p h q2", h=16), r, op=ALU.subtract),
                            reads=(rcsb[q], rcsT2[p]), writes=(rcsb[q],))
                        kb.cop("vector", lambda e, b=csb[q]: e.tensor_scalar(
                            b[:, :], b[:, :], 4.70, None, op0=ALU.min),
                            reads=(rcsb[q],), writes=(rcsb[q],))
                        kb.cop("scalar", lambda e, b=csb[q], o=Eb[q]: e.activation(o[:, :], b[:, :], AF.Exp),
                            reads=(rcsb[q],), writes=(rEb[q],))
                        kb.do("sync", lambda e, gg=g, c=ci, b=e2b[q]: e.dma_start(
                            out=b[:, :],
                            in_=_ap(s_gexp.tensor, c * NH * L + gg * 16 * L, [[0, 128], [1, 16 * L]])),
                            reads=(rD["s_gexp"],), writes=(re2b[q],), sem=ld(f"e2b{q}"), d=16)
                    seg_chain(0)
                    seg_chain(1)
                    for g in range(NG):
                        q = g % 3
                        if g + 2 < NG:
                            seg_chain(g + 2)
                        q2 = g % 2
                        cb_bank, rcb_bank = pB[q2], rpB[q2]
                        kb.cop("tensor", lambda e, gg=g, b=cb_bank, i=p: e.matmul(
                            b[:, 0:L], bcw[i][:, gg * L:(gg + 1) * L],
                            bcw[i][:, 8 * L + gg * L:8 * L + (gg + 1) * L],
                            start=True, stop=True),
                            reads=(rbcw[p],), writes=(rcb_bank,))
                        kb.cop("vector", lambda e, b=cb_bank, o=cbm[q2]: e.tensor_tensor(
                            o[:, :], b[:, 0:L], k_tri[:, :], op=ALU.mult),
                            reads=(rcb_bank, rK), writes=(rcbm[q2],))
                        repcb = _ap(cbm[q2], 0, [[L, 128], [0, 16], [1, L]])
                        kb.cop("vector", lambda e, r=repcb, i=q2, i3=q: e.tensor_tensor(
                            scT[i][:].rearrange("p (h q2) -> p h q2", h=16), r,
                            Eb[i3][:].rearrange("p (h q2) -> p h q2", h=16), op=ALU.mult),
                            reads=(rcbm[q2], rEb[q]), writes=(rscT[q2],))
                        repc = _ap(bcw[p], 8 * L + g * L, [[16 * L, 128], [0, 16], [1, L]])
                        kb.cop("vector", lambda e, r=repc, i=q2, i3=q: e.tensor_tensor(
                            cgb[i][:].rearrange("p (h q2) -> p h q2", h=16), r,
                            e2b[i3][:].rearrange("p (h q2) -> p h q2", h=16), op=ALU.mult),
                            reads=(rbcw[p], re2b[q]), writes=(rcgb[q2],))
                        for half in range(2):
                            yb = pA[2 + half]
                            ryb = rpA[2 + half]
                            for jp4 in range(4):
                                jp = half * 4 + jp4
                                j = g * 8 + jp
                                for hh in (0, 1):
                                    h = 2 * j + hh
                                    hl = h - g * 16
                                    kb.do("tensor", lambda e, h_=h, hl_=hl, o=64 * hh, b=yb,
                                          c4=jp4 * L, i=q2: e.matmul(
                                        b[o:o + 64, c4:c4 + L],
                                        x_tm[:, h_ * 64:(h_ + 1) * 64],
                                        scT[i][:, hl_ * L:(hl_ + 1) * L],
                                        start=True, stop=False, tile_position=(0, o),
                                        skip_group_check=True),
                                        reads=(rx_tm, rscT[q2]), writes=(ryb,), sem=None)
                                for hh in (0, 1):
                                    h = 2 * j + hh
                                    hl = h - g * 16
                                    kb.do("tensor", lambda e, h_=h, hl_=hl, o=64 * hh, b=yb,
                                          c4=jp4 * L, i=q2: e.matmul(
                                        b[o:o + 64, c4:c4 + L],
                                        prevb[:, h_ * 64:(h_ + 1) * 64],
                                        cgb[i][:, hl_ * L:(hl_ + 1) * L],
                                        start=False, stop=False, tile_position=(0, o),
                                        skip_group_check=True),
                                        reads=(rprevb, rcgb[q2]), writes=(ryb,), sem=None)
                                for hh in (0, 1):
                                    h = 2 * j + hh
                                    kb.do("tensor", lambda e, h_=h, o=64 * hh, b=yb,
                                          c4=jp4 * L: e.matmul(
                                        b[o:o + 64, c4:c4 + L],
                                        xD[:, h_ * 64:(h_ + 1) * 64],
                                        k_identb[:, :],
                                        start=False, stop=True, tile_position=(0, o),
                                        skip_group_check=True),
                                        reads=(rxD, rK), writes=(ryb,),
                                        sem=(kb.ms["tensor"] if (hh == 1 and jp4 == 3) else None))
                            j0 = g * 8 + half * 4
                            ys, rys = ystg[half], rystg[half]
                            kb.cop("vector", lambda e, b=yb, s=ys: e.tensor_copy(s[:, :], b[:, :]),
                                reads=(ryb,), writes=(rys,))
                            dst = _ap(s_ypart.tensor, t0 // L * 128 * IC * L + j0 * L,
                                      [[IC * L, 128], [1, 512]])
                            kb.do("sync", lambda e, dd=dst, s=ys: e.dma_start(out=dd, in_=s[:, :]),
                                reads=(rys,), writes=(rD["s_ypart"],), sem=sst(f"ystg{half}"), d=16)
                kb.do("sync", lambda e: e.dma_start(out=scr["s_carry"], in_=carry[:, :]),
                      reads=(rcarry,), writes=(rD["s_carry"],), sem=sst("carry"), d=16)
            kb.barrier()

        sdram("s_dt", [128, T], F32)
        rD["s_dt"] = kb.res("d_s_dt")
        sdram("s_ln2", [D, T], BF16)
        rD["s_ln2"] = kb.res("d_s_ln2")

        def carry_exchange(carrymb, rcarrymb, sk):
            carrym = sb(sk, "carrym", [128, INTER]); rcarrym = kb.res("carrym")
            if n_pairs > 0:
                groups = [[2 * i, 2 * i + 1] for i in range(n_pairs)]
                kb.do("gpsimd", lambda e, g=groups: e.collective_compute(
                    "AllGather", ALU.bypass, replica_groups=g,
                    ins=[scr["s_carry"]], outs=[scr["s_carryag"]]),
                    reads=(rD["s_carry"],), writes=(rD["s_carryag"],),
                    sem=kb.sem("cc"), d=1)
                kb.do("sync", lambda e: e.dma_start(
                    out=carrym[:, :], in_=_ap(scr["s_carryag"].tensor, 0, [[INTER, 128], [1, INTER]])),
                    reads=(rD["s_carryag"],), writes=(rcarrym,), sem=ld("carrym"), d=16)
                kb.cop("vector", lambda e: e.tensor_scalar(
                    carrym[:, :], carrym[:, :], k_isodd[:, 0:1], None, op0=ALU.mult),
                    reads=(rcarrym, rK), writes=(rcarrym,))
            else:
                kb.cop("vector", lambda e: e.memset(carrym[:, :], 0.0), writes=(rcarrym,))
            kb.cop("vector", lambda e: e.tensor_copy(carrymb[:, :], carrym[:, :]),
                reads=(rcarrym,), writes=(rcarrymb,))

        def phase_b_gating():
            with ExitStack() as sk:
                carrymb = sb(sk, "carrymb", [128, INTER], BF16); rcarrymb = kb.res("carrymb")
                with ExitStack() as sk2:
                    carry_exchange(carrymb, rcarrymb, sk2)
                    kb.barrier()
                ybuf = [sb(sk, f"ybuf2_{i}", [128, IC * L]) for i in range(2)]
                rybuf = [kb.res(f"ybuf2_{i}") for i in range(2)]
                gatec_ = [sb(sk, f"gatec{i}", [128, IC * L], BF16) for i in range(2)]
                rgatec_ = [kb.res(f"gatec{i}") for i in range(2)]
                bcwC = [sb(sk, f"bcwC2_{i}", [128, 8 * L], BF16) for i in range(2)]
                rbcwC = [kb.res(f"bcwC2_{i}") for i in range(2)]
                e2b = [sb(sk, f"e2b2_{i}", [128, 16 * L], BF16) for i in range(3)]
                re2b = [kb.res(f"e2b2_{i}") for i in range(3)]
                cgb = [sb(sk, f"cgb2_{i}", [128, 16 * L], BF16) for i in range(2)]
                rcgb = [kb.res(f"cgb2_{i}") for i in range(2)]
                ysq = sb(sk, "ysq", [128, IC * L], BF16); rysq = kb.res("ysq")
                nrm = sb(sk, "nrm", [128, IC * L], BF16); rnrm = kb.res("nrm")
                ssrow = sb(sk, "ssrow", [1, 8 * L]); rssrow = kb.res("ssrow")
                ssrb = sb(sk, "ssrb", [1, 8 * L], BF16); rssrb = kb.res("ssrb")

                for ci in range(NT):
                    t0 = ci * L
                    p = ci % 2
                    yb_, ryb_ = ybuf[p], rybuf[p]
                    kb.do("sync", lambda e, c=ci, b=yb_: e.dma_start(
                        out=b[:, :], in_=_ap(s_ypart.tensor, c * 128 * IC * L, [[IC * L, 128], [1, IC * L]])),
                        reads=(rD["s_ypart"],), writes=(ryb_,), sem=ld(f"ybuf2_{p}"), d=16)
                    gatec, rgatec = gatec_[p], rgatec_[p]
                    kb.do("sync", lambda e, t=t0, gc=gatec: e.dma_start(
                        out=gc[:].rearrange("p (c t) -> p c t", c=IC),
                        in_=_ap(s_gate.tensor, t, [[T, 128], [128 * T, IC], [1, L]])),
                        reads=(rD["s_gate"],), writes=(rgatec,), sem=ld(f"gatec{p}"), d=16)
                    kb.do("sync", lambda e, c=ci, b=bcwC[p]: e.dma_start(
                        out=b[:, :], in_=_ap(s_bcf.tensor, c * 128 * 16 * L + 8 * L, [[16 * L, 128], [1, 8 * L]])),
                        reads=(rD["s_bcf"],), writes=(rbcwC[p],), sem=ld(f"bcwC2_{p}"), d=16)
                    def cg_chain(g):
                        q3 = g % 3
                        kb.do("sync", lambda e, gg=g, c=ci, b=e2b[q3]: e.dma_start(
                            out=b[:, :],
                            in_=_ap(s_gexp2.tensor, c * NH * L + gg * 16 * L, [[0, 128], [1, 16 * L]])),
                            reads=(rD["s_gexp2"],), writes=(re2b[q3],), sem=ld(f"e2b2_{q3}"), d=16)
                        repc = _ap(bcwC[p], g * L, [[8 * L, 128], [0, 16], [1, L]])
                        kb.cop("gpsimd", lambda e, r=repc, i=g % 2, i3=q3: e.tensor_tensor(
                            cgb[i][:].rearrange("p (h q2) -> p h q2", h=16), r,
                            e2b[i3][:].rearrange("p (h q2) -> p h q2", h=16), op=ALU.mult),
                            reads=(rbcwC[p], re2b[q3]), writes=(rcgb[i := g % 2] if False else rcgb[g % 2],))
                    cg_chain(0)
                    cg_chain(1)
                    for g in range(NG):
                        q = g % 2
                        if g + 2 < NG:
                            cg_chain(g + 2)
                        for half in range(2):
                            pbk = pA[1 + half]
                            rpbk = rpA[1 + half]
                            for jp4 in range(4):
                                jp = half * 4 + jp4
                                j = g * 8 + jp
                                for hh in (0, 1):
                                    h = 2 * j + hh
                                    hl = h - g * 16
                                    kb.do("tensor", lambda e, h_=h, hl_=hl, o=64 * hh, b=pbk,
                                          c4=jp4 * L, i=q: e.matmul(
                                        b[o:o + 64, c4:c4 + L],
                                        carrymb[:, h_ * 64:(h_ + 1) * 64],
                                        cgb[i][:, hl_ * L:(hl_ + 1) * L],
                                        start=True, stop=True, tile_position=(0, o),
                                        skip_group_check=True),
                                        reads=(rcarrymb, rcgb[q]), writes=(rpbk,),
                                        sem=(kb.ms["tensor"] if (hh == 1 and jp4 == 3) else None))
                            j0 = g * 8 + half * 4
                            kb.cop("vector", lambda e, b=pbk, y=yb_, j0_=j0: e.tensor_tensor(
                                y[:, j0_ * L:(j0_ + 4) * L], y[:, j0_ * L:(j0_ + 4) * L],
                                b[:, :], op=ALU.add),
                                reads=(ryb_, rpbk), writes=(ryb_,))
                    kb.cop("vector", lambda e, y=yb_, gc=gatec: e.tensor_tensor(
                        y[:, :], y[:, :], gc[:, :], op=ALU.mult),
                        reads=(ryb_, rgatec), writes=(ryb_,))
                    kb.cop("scalar", lambda e, y=yb_: e.activation(ysq[:, :], y[:, :], AF.Square),
                        reads=(ryb_,), writes=(rysq,))
                    for g in range(NG):
                        for jj in range(8):
                            kb.do("tensor", lambda e, gg=g, j2=jj: e.matmul(
                                pB[0][0:1, gg * L:(gg + 1) * L], k_onesb[:, 0:1],
                                ysq[:, (gg * 8 + j2) * L:(gg * 8 + j2 + 1) * L],
                                start=(j2 == 0), stop=(j2 == 7), skip_group_check=True),
                                reads=(rysq, rK), writes=(rpB[0],),
                                sem=(kb.ms["tensor"] if jj == 7 else None))
                    kb.cop("vector", lambda e: e.tensor_scalar(
                        ssrow[:, :], pB[0][0:1, 0:8 * L], 1.0 / 1024.0, None, op0=ALU.mult),
                        reads=(rpB[0],), writes=(rssrow,))
                    kb.cop("scalar", lambda e: e.activation(
                        ssrow[:, :], ssrow[:, :], AF.Sqrt, bias=k_eps[0:1, 0:1]),
                        reads=(rssrow, rK), writes=(rssrow,))
                    kb.cop("vector", lambda e: e.reciprocal(ssrow[:, :], ssrow[:, :]),
                        reads=(rssrow,), writes=(rssrow,))
                    kb.cop("vector", lambda e: e.tensor_copy(ssrb[:, :], ssrow[:, :]),
                        reads=(rssrow,), writes=(rssrb,))
                    for g in range(NG):
                        kb.cop("tensor", lambda e, gg=g: e.matmul(
                            pB[1][:, gg * L:(gg + 1) * L], k_onesrow[:, :],
                            ssrb[0:1, gg * L:(gg + 1) * L], start=True, stop=True,
                            skip_group_check=True),
                            reads=(rssrb, rK), writes=(rpB[1],))
                    for g in range(NG):
                        repnw = _ap(k_nwexp, 8 * g, [[IC, 128], [1, 8], [0, L]])
                        kb.cop("vector", lambda e, gg=g, r=repnw, y=yb_: e.tensor_tensor(
                            y[:, gg * 1024:(gg + 1) * 1024].rearrange("p (c t) -> p c t", c=8),
                            y[:, gg * 1024:(gg + 1) * 1024].rearrange("p (c t) -> p c t", c=8),
                            r, op=ALU.mult),
                            reads=(ryb_, rK), writes=(ryb_,))
                        reprs = _ap(pB[1], g * L, [[1024, 128], [0, 8], [1, L]])
                        kb.cop("vector", lambda e, gg=g, r=reprs, y=yb_: e.tensor_tensor(
                            nrm[:, gg * 1024:(gg + 1) * 1024].rearrange("p (c t) -> p c t", c=8),
                            y[:, gg * 1024:(gg + 1) * 1024].rearrange("p (c t) -> p c t", c=8),
                            r, op=ALU.mult),
                            reads=(ryb_, rpB[1]), writes=(rnrm,))
                    kb.do("sync", lambda e, t=t0: e.dma_start(
                        out=_ap(s_normed.tensor, t, [[T, 128], [128 * T, IC], [1, L]]),
                        in_=nrm[:].rearrange("p (c t) -> p c t", c=IC)),
                        reads=(rnrm,), writes=(rD["s_normed"],), sem=sst("nrm"), d=16)
            kb.barrier()

        def emit_mm_stage(rhs_dram, rrhs, rhs_nch, wsrc, noc, out_cb, tiles_per_pass=2,
                          wsrc_is_bf16=False, tag="mm"):
            with ExitStack() as sk:
                slab = [sb(sk, f"{tag}rs{i}", [128, rhs_nch * W], BF16) for i in range(tiles_per_pass)]
                rslab = [kb.res(f"{tag}rs{i}") for i in range(tiles_per_pass)]
                wsl = [sb(sk, f"{tag}ws{i}", [128, rhs_nch * 128], BF16) for i in range(3)]
                rwsl = [kb.res(f"{tag}ws{i}") for i in range(3)]
                npass = (NTILE + tiles_per_pass - 1) // tiles_per_pass
                KT = rhs_nch * 128
                for pi in range(npass):
                    tiles = [t for t in range(pi * tiles_per_pass, min((pi + 1) * tiles_per_pass, NTILE))]
                    for i, tj in enumerate(tiles):
                        ap = _ap(rhs_dram.tensor, tj * W, [[T, 128], [128 * T, rhs_nch], [1, W]])
                        kb.do("sync", lambda e, s=ap, d2=slab[i]: e.dma_start(
                            out=d2[:].rearrange("p (c t) -> p c t", c=rhs_nch), in_=s),
                            reads=(rrhs,), writes=(rslab[i],), sem=ld(f"{tag}rs{i}"), d=16)
                    for o in range(noc):
                        sl, rsl = wsl[o % 3], rwsl[o % 3]
                        ap = _ap(wsrc.tensor, o * 128, [[noc * 128, 128], [128 * noc * 128, rhs_nch], [1, 128]])
                        eng = "sync" if wsrc_is_bf16 else "gpsimd"
                        kb.do(eng, lambda e, s=ap, d2=sl: e.dma_start(
                            out=d2[:].rearrange("p (c o) -> p c o", c=rhs_nch), in_=s),
                            reads=(), writes=(rsl,), sem=ld(f"{tag}ws{o % 3}"), d=16)
                        for i, tj in enumerate(tiles):
                            bi = (o * len(tiles) + i) % 3
                            bank, rbank = pA[bi], rpA[bi]
                            for kc in range(rhs_nch):
                                kb.do("tensor", lambda e, k=kc, b=bank, s=sl, x=slab[i], n=rhs_nch: e.matmul(
                                    b[:, :W], s[:, k * 128:(k + 1) * 128], x[:, k * W:(k + 1) * W],
                                    start=(k == 0), stop=(k == n - 1)),
                                    reads=(rsl, rslab[i]), writes=(rbank,),
                                    sem=(kb.ms["tensor"] if kc == rhs_nch - 1 else None))
                            out_cb(o, tj, bank, rbank, sk)

        def out_proj_stage():
            stg8 = {}
            def s8_evac(o, tj, bank, rbank, sk):
                if "b" not in stg8:
                    stg8["b"] = [sb(sk, f"stg8{i}", [128, W], BF16) for i in range(3)]
                    stg8["r"] = [kb.res(f"stg8{i}") for i in range(3)]
                    stg8["f"] = [sb(sk, f"stg8f{i}", [128, W]) for i in range(3)]
                    stg8["rf"] = [kb.res(f"stg8f{i}") for i in range(3)]
                si = (o * NTILE + tj) % 3
                fb, rfb = stg8["f"][si], stg8["rf"][si]
                kb.cop("scalar", lambda e, b=bank, s=fb: e.activation(
                    s[:, :W], b[:, :W], AF.Identity),
                    reads=(rbank,), writes=(rfb,))
                dst = _ap(s_att.tensor, o * 128 * T + tj * W, [[T, 128], [1, W]])
                kb.do("sync", lambda e, dd=dst, s=fb: e.dma_start(out=dd, in_=s[:, :W]),
                      reads=(rfb,), writes=(rD["s_att"],), sem=sst(f"stg8f{si}"), d=16)
            emit_mm_stage(s_normed, rD["s_normed"], IC, wout_t, DC, s8_evac, tiles_per_pass=2, tag="op")
            kb.barrier()
            with ExitStack() as sk:
                lb = alloc_ln_bufs(sk, "_s8", with_out=True)
                for tj in range(NTILE):
                    emit_ln(lb, s_att, rD["s_att"], T, tj * W, W, DC, "resid",
                            resid_dram=(xin, rD["xin"]), dst_dram=s_x2, rdst=rD["s_x2"])
            kb.barrier()

        def ffn_stage():
            with ExitStack() as sk:
                lb = alloc_ln_bufs(sk, "_s9")
                ln2sb = sb(sk, "ln2sb", [128, DC * W], BF16); rln2sb = kb.res("ln2sb")
                for tj in range(NTILE):
                    emit_ln(lb, s_x2, rD["s_x2"], T, tj * W, W, DC, "bf16",
                            out_sb=ln2sb, rout=rln2sb)
                    dst = _ap(scr["s_ln2"].tensor, tj * W, [[T, 128], [128 * T, DC], [1, W]])
                    kb.do("sync", lambda e, dd=dst: e.dma_start(
                        out=dd, in_=ln2sb[:].rearrange("p (c t) -> p c t", c=DC)),
                        reads=(rln2sb,), writes=(rD["s_ln2"],), sem=sst("ln2sb"), d=16)
            kb.barrier()
            with ExitStack() as sk:
                k_bstack = sb(sk, "k_bstack", [128, D], BF16)
                rbstack = kb.res("k_bstack")
                kb.do("gpsimd", lambda e: e.dma_start(out=k_bstack[:, :], in_=bstack[:, :]),
                      reads=(), writes=(rbstack,), sem=ld("k_bstack"), d=16)
                slab = sb(sk, "f9rs", [128, DC * W], BF16); rslab = kb.res("f9rs")
                wsl = [sb(sk, f"f9ws{i}", [128, DC * 128], BF16) for i in range(3)]
                rwsl = [kb.res(f"f9ws{i}") for i in range(3)]
                hbuf = sb(sk, "hbuf", [128, W], BF16); rhbuf = kb.res("hbuf")
                hwb = sb(sk, "hwb", [128, W], BF16); rhwb = kb.res("hwb")
                glb = sb(sk, "glb", [128, W]); rglb = kb.res("glb")
                gt = sb(sk, "gt", [128, 128]); rgt = kb.res("gt")
                e8 = sb(sk, "e8", [128, 8]); re8 = kb.res("e8")
                gcol = sb(sk, "gcol", [128, 4]); rgcol = kb.res("gcol")
                wn = sb(sk, "wn", [128, 128]); rwn = kb.res("wn")

                wfmb = sb(sk, "wfmb", [8, 128], BF16); rwfmb = kb.res("wfmb")
                stgf = [sb(sk, f"stg9f{i}", [128, W]) for i in range(3)]
                rstgf = [kb.res(f"stg9f{i}") for i in range(3)]
                kb.cop("vector", lambda e: e.memset(wn[:, :], 0.0), writes=(rwn,))
                order = [32, 33] + list(range(32))
                for tj in range(NTILE):
                    ap = _ap(scr["s_ln2"].tensor, tj * W, [[T, 128], [128 * T, DC], [1, W]])
                    kb.do("sync", lambda e, s=ap: e.dma_start(
                        out=slab[:].rearrange("p (c t) -> p c t", c=DC), in_=s),
                        reads=(rD["s_ln2"],), writes=(rslab,), sem=ld("f9rs"), d=16)
                    for oi, o in enumerate(order):
                        sl, rsl = wsl[oi % 3], rwsl[oi % 3]
                        ap = _ap(wffn1_t.tensor, o * 128, [[FFN1, 128], [128 * FFN1, DC], [1, 128]])
                        kb.do("gpsimd", lambda e, s=ap, d2=sl: e.dma_start(
                            out=d2[:].rearrange("p (c o) -> p c o", c=DC), in_=s),
                            reads=(), writes=(rsl,), sem=ld(f"f9ws{oi % 3}"), d=16)
                        bank, rbank = pA[oi % 2], rpA[oi % 2]
                        for kc in range(DC):
                            last = (kc == DC - 1) and (o >= 32)
                            kb.do("tensor", lambda e, k=kc, b=bank, s=sl, lt=last: e.matmul(
                                b[:, :W], s[:, k * 128:(k + 1) * 128], slab[:, k * W:(k + 1) * W],
                                start=(k == 0), stop=lt),
                                reads=(rsl, rslab), writes=(rbank,),
                                sem=(kb.ms["tensor"] if last else None))
                        if o < 32:
                            kb.do("tensor", lambda e, b=bank, o_=o: e.matmul(
                                b[:, :W], k_bstack[:, o_ * 128:(o_ + 1) * 128], hwb[:, :W],
                                start=False, stop=True),
                                reads=(rbstack, rhwb), writes=(rbank,), sem=kb.ms["tensor"])
                            si = oi % 3
                            kb.cop("scalar", lambda e, b=bank, s=stgf[si], o_=o: e.activation(
                                s[:, :W], b[:, :W], AF.Identity, bias=k_bbase[:, o_:o_ + 1]),
                                reads=(rbank, rK), writes=(rstgf[si],))
                            dst = _ap(s_ffn.tensor, o * 128 * T + tj * W, [[T, 128], [1, W]])
                            kb.do("sync", lambda e, dd=dst, s=stgf[si]: e.dma_start(out=dd, in_=s[:, :W]),
                                  reads=(rstgf[si],), writes=(rD["s_ffn"],), sem=sst(f"stg9f{si}"), d=16)
                        elif o == 32:
                            kb.cop("scalar", lambda e, b=bank: e.activation(
                                hbuf[:, :W], b[:, :W], AF.Identity),
                                reads=(rbank,), writes=(rhbuf,))
                        else:
                            kb.cop("scalar", lambda e, b=bank: e.activation(
                                glb[:, :W], b[:, :W], AF.Identity),
                                reads=(rbank,), writes=(rglb,))
                            for w4 in range(W // 128):
                                kb.cop("tensor", lambda e, w_=w4: e.transpose(
                                    pA[2][:, 0:128], glb[:, w_ * 128:(w_ + 1) * 128], k_identf[:, :]),
                                    reads=(rglb, rK), writes=(rpA[2],))
                                kb.cop("vector", lambda e: e.tensor_copy(gt[:, :], pA[2][:, 0:128]),
                                    reads=(rpA[2],), writes=(rgt,))
                                kb.cop("vector", lambda e: e.reduce_max(
                                    gcol[:, 0:1], gt[:, 0:8], axis=mybir.AxisListType.X),
                                    reads=(rgt,), writes=(rgcol,))
                                kb.cop("vector", lambda e: e.tensor_scalar(
                                    e8[:, :], gt[:, 0:8], gcol[:, 0:1], None, op0=ALU.subtract),
                                    reads=(rgt, rgcol), writes=(re8,))
                                kb.cop("scalar", lambda e: e.activation(e8[:, :], e8[:, :], AF.Exp),
                                    reads=(re8,), writes=(re8,))
                                kb.cop("vector", lambda e: e.reduce_sum(
                                    gcol[:, 1:2], e8[:, :], axis=mybir.AxisListType.X),
                                    reads=(re8,), writes=(rgcol,))
                                kb.cop("vector", lambda e: e.reciprocal(gcol[:, 1:2], gcol[:, 1:2]),
                                    reads=(rgcol,), writes=(rgcol,))
                                kb.cop("scalar", lambda e: e.activation(
                                    gcol[:, 2:3], gt[:, 8:9], AF.Sigmoid, bias=k_bthr[:, 0:1]),
                                    reads=(rgt, rK), writes=(rgcol,))
                                kb.cop("vector", lambda e: e.tensor_scalar(
                                    gcol[:, 2:3], gcol[:, 2:3], 0.125, None, op0=ALU.mult),
                                    reads=(rgcol,), writes=(rgcol,))
                                rept = _ap(gcol, 2, [[4, 128], [0, 8]])
                                kb.cop("vector", lambda e, r=rept: e.scalar_tensor_tensor(
                                    e8[:, :], e8[:, :], gcol[:, 1:2], r,
                                    op0=ALU.mult, op1=ALU.subtract),
                                    reads=(re8, rgcol), writes=(re8,))
                                kb.cop("vector", lambda e: e.tensor_scalar(
                                    e8[:, :], e8[:, :], 0.0, None, op0=ALU.max),
                                    reads=(re8,), writes=(re8,))
                                kb.cop("vector", lambda e: e.reduce_sum(
                                    gcol[:, 0:1], e8[:, :], axis=mybir.AxisListType.X),
                                    reads=(re8,), writes=(rgcol,))
                                kb.cop("vector", lambda e: e.tensor_scalar(
                                    gcol[:, 3:4], gcol[:, 0:1], 0.0, None, op0=ALU.is_equal),
                                    reads=(rgcol,), writes=(rgcol,))
                                kb.cop("vector", lambda e: e.tensor_tensor(
                                    gcol[:, 0:1], gcol[:, 0:1], gcol[:, 3:4], op=ALU.add),
                                    reads=(rgcol,), writes=(rgcol,))
                                kb.cop("vector", lambda e: e.reciprocal(gcol[:, 0:1], gcol[:, 0:1]),
                                    reads=(rgcol,), writes=(rgcol,))
                                kb.cop("vector", lambda e: e.tensor_scalar(
                                    wn[:, 0:8], e8[:, :], gcol[:, 0:1], 2.0,
                                    op0=ALU.mult, op1=ALU.mult),
                                    reads=(re8, rgcol), writes=(rwn,))
                                kb.cop("tensor", lambda e: e.transpose(
                                    pA[2][:, 128:256], wn[:, :], k_identf[:, :]),
                                    reads=(rwn, rK), writes=(rpA[2],))
                                kb.cop("vector", lambda e: e.tensor_copy(wfmb[:, :], pA[2][0:8, 128:256]),
                                    reads=(rpA[2],), writes=(rwfmb,))
                                kb.cop("tensor", lambda e: e.matmul(
                                    pA[2][:, 256:384], k_mmat[:, :], wfmb[:, :],
                                    start=True, stop=True),
                                    reads=(rwfmb, rK), writes=(rpA[2],))
                                kb.cop("vector", lambda e, w_=w4: e.tensor_tensor(
                                    hwb[:, w_ * 128:(w_ + 1) * 128], hbuf[:, w_ * 128:(w_ + 1) * 128],
                                    pA[2][:, 256:384], op=ALU.mult),
                                    reads=(rhbuf, rpA[2]), writes=(rhwb,))
            kb.barrier()
            with ExitStack() as sk:
                lb = alloc_ln_bufs(sk, "_s9b", with_out=True)
                for tj in range(NTILE):
                    emit_ln(lb, s_ffn, rD["s_ffn"], T, tj * W, W, DC, "resid",
                            resid_dram=(scr["s_x2"], rD["s_x2"]), dst_dram=out_t, rdst=rD["out_t"])
            kb.barrier()

        s0()
        mixer_front()
        if upto not in ("inproj", "conv"):
            ssd_phase_a()
        if upto in ("all", "normed"):
            phase_b_gating()
        if upto == "all":
            out_proj_stage()
            ffn_stage()

        kb.barrier()

        block = st.enter_context(nc.Block())
        kb.playback(block)

    return nc


def host_prepare(inputs, T=2048):
    x = np.asarray(inputs["L0_attn_hidden_states"])
    B, S, _ = x.shape
    win = np.asarray(inputs["L0_attn_in_proj_w"])
    wout = np.asarray(inputs["L0_attn_out_proj_w"])
    convw_ = np.ascontiguousarray(np.asarray(inputs["L0_attn_conv_w"]).reshape(CC, 128, K)
                                  .transpose(1, 0, 2).reshape(128, CC * K).astype(np.float32))
    convb_ = np.ascontiguousarray(np.asarray(inputs["L0_attn_conv_b"]).reshape(CC, 128).T.astype(np.float32))
    dtb = np.asarray(inputs["L0_attn_dt_bias"]).reshape(NH, 1).astype(np.float32)
    Dd = np.asarray(inputs["L0_attn_D"]).reshape(NH)
    nw = np.asarray(inputs["L0_attn_norm_w"]).reshape(INTER)
    wb = np.asarray(inputs["L0_ffn_W_base"])
    bb = np.ascontiguousarray(np.asarray(inputs["L0_ffn_b_base"]).reshape(DC, 128).T.astype(np.float32))
    wg = np.asarray(inputs["L0_ffn_W_gate"])
    wt = np.asarray(inputs["L0_ffn_W_thr"]).reshape(1, D)
    bt = np.asarray(inputs["L0_ffn_b_thr"]).reshape(1)
    aw = np.asarray(inputs["L0_ffn_A_w"])
    bw = np.asarray(inputs["L0_ffn_B_w"])

    win_t = np.ascontiguousarray(win.T.astype(np.float32))
    wout_t = np.ascontiguousarray(wout.T.astype(np.float32))
    ffn1 = np.concatenate([wb, aw.reshape(128, D), wg, wt,
                           np.zeros((FFN1 - 4233, D), np.float32)], axis=0).astype(np.float32)
    wffn1_t = np.ascontiguousarray(ffn1.T)
    bstack_ = np.ascontiguousarray(np.transpose(bw, (0, 2, 1)).reshape(128, D).astype(np.float32))
    acolv = -(np.arange(1, NH + 1, dtype=np.float32)).reshape(NH, 1)
    dexpv = np.ascontiguousarray(np.repeat(Dd, HD).reshape(IC, 128).T.astype(np.float32))
    nwexpv = np.ascontiguousarray(nw.reshape(IC, 128).T.astype(np.float32))
    bthrv = np.full((128, 1), bt[0], np.float32)
    triv = (np.arange(L)[None, :] >= np.arange(L)[:, None]).astype(bf16)
    mmatv = np.zeros((8, 128), bf16)
    for e_ in range(8):
        mmatv[e_, e_ * 16:(e_ + 1) * 16] = bf16(1.0)

    shared = dict(win_t=win_t, wout_t=wout_t, wffn1_t=wffn1_t, bstack=bstack_,
                  convw=convw_, convb=convb_, dtbias=dtb, acol=acolv, dexp=dexpv,
                  nwexp=nwexpv, bbase=bb, bthr=bthrv, tri=triv, mmat=mmatv,
                  identf=np.eye(128, dtype=np.float32),
                  dtm=np.ascontiguousarray(np.tile(Dd.astype(np.float32), (128, 1))),
                  identb=np.eye(128).astype(bf16),
                  onesb=np.ones((128, 1), bf16), onesrow=np.ones((1, 128), bf16))
    in_maps = []
    halfs = S // T
    for core in range(B * halfs):
        b, hh = core // halfs, core % halfs
        xt = np.ascontiguousarray(x[b, hh * T:(hh + 1) * T, :].T.astype(np.float32))
        if hh == 0:
            xh = np.zeros((D, 3), np.float32)
        else:
            xh = np.ascontiguousarray(x[b, hh * T - 3:hh * T, :].T.astype(np.float32))
        iso = np.full((128, 1), 1.0 if hh % 2 == 1 else 0.0, np.float32)
        m = dict(shared)
        m.update(xin=xt, xhalo=xh, isodd=iso)
        in_maps.append(m)
    return in_maps


def kernel(**inputs):
    from concourse.bass_utils import run_bass_kernel_spmd
    T = 2048
    in_maps = host_prepare(inputs, T)
    nc = build_program(T=T, n_pairs=4)
    res = run_bass_kernel_spmd(nc, in_maps, core_ids=list(range(8)))
    x = np.asarray(inputs["L0_attn_hidden_states"])
    B, S, _ = x.shape
    out = np.empty((B, S, D), np.float32)
    halfs = S // T
    for core in range(B * halfs):
        b, hh = core // halfs, core % halfs
        out[b, hh * T:(hh + 1) * T, :] = res.results[core]["out_t"].T
    return out
